# version 15
# baseline (speedup 1.0000x reference)
import contextlib
import numpy as np

import concourse.bass as bass
import concourse.mybir as mybir
from concourse.bass_utils import run_bass_kernel_spmd
from concourse.alu_op_type import AluOpType as ALU

F32 = mybir.dt.float32
BF16 = mybir.dt.bfloat16
AF = mybir.ActivationFunctionType
AX = mybir.AxisListType

NCORE = 8
T = 1024
NG = 2
G = 512
NTB = 4
NH = 16
DM = 2048
FC = 16
IN = 8192
UC = 64
TF = 256
NTF = 2

ENGS = ("sync", "tensor", "vector", "scalar", "gpsimd")


class Prog:

    def __init__(self, sems):
        self.ops = {e: [] for e in ENGS}
        self.sems = sems
        self.cnt = {k: 0 for k in sems}

    def emit(self, eng, fn, incs=()):
        caps = [(self.sems[s], n) for s, n in incs]

        def run(e, fn=fn, caps=caps):
            ins = fn(e)
            for h, n in caps:
                ins = ins.then_inc(h, n)

        self.ops[eng].append(run)
        for s, n in incs:
            self.cnt[s] += n

    def tok(self, sem):
        return (sem, self.cnt[sem])

    def wait(self, eng, sem, val=None):
        v = self.cnt[sem] if val is None else val
        if v <= 0:
            return
        h = self.sems[sem]
        self.ops[eng].append(lambda e, h=h, v=v: e.wait_ge(h, v))

    def wait_tok(self, eng, tok):
        if tok is not None:
            self.wait(eng, tok[0], tok[1])


def _hadamard128():
    i = np.arange(128)
    a = (i[:, None] & i[None, :])
    pc = np.zeros((128, 128), dtype=np.int64)
    for b in range(7):
        pc += (a >> b) & 1
    return np.where(pc % 2 == 0, 1.0, -1.0).astype(np.float32)


def sub_ap(view, extra_off, dims):
    return bass.AP(view.tensor, view.offset + extra_off, [list(view.ap[0])] + [list(d) for d in dims])


def build_nc(taps=()):
    nc = bass.Bass("TRN2", target_bir_lowering=False, detect_race_conditions=False)

    x_d = nc.dram_tensor("x", [T, DM], F32, kind="ExternalInput").ap()
    y_d = nc.dram_tensor("y", [T, DM], F32, kind="ExternalOutput").ap()
    wqkv_d = nc.dram_tensor("wqkv", [NH, 128, FC, 384], F32, kind="ExternalInput").ap()
    bqkv_d = nc.dram_tensor("bqkv", [NH, 128, 384], F32, kind="ExternalInput").ap()
    wog_d = nc.dram_tensor("wog", [NH, 128, FC, 128], F32, kind="ExternalInput").ap()
    wo_d = nc.dram_tensor("wo", [4, 128, FC, 512], F32, kind="ExternalInput").ap()
    wb_d = nc.dram_tensor("wb", [128, FC, 16], F32, kind="ExternalInput").ap()
    wug_d = nc.dram_tensor("wug", [UC, 128, FC, 2, 128], F32, kind="ExternalInput").ap()
    wdn_d = nc.dram_tensor("wdn", [4, 8, 128, 8, 512], F32, kind="ExternalInput").ap()
    bo_d = nc.dram_tensor("bo", [128, DM], F32, kind="ExternalInput").ap()
    wscdp_d = nc.dram_tensor("wscdp", [128, DM], F32, kind="ExternalInput").ap()
    bb4_d = nc.dram_tensor("bb4", [128, 64], F32, kind="ExternalInput").ap()
    suug_d = nc.dram_tensor("suug", [128, 16], F32, kind="ExternalInput").ap()
    wsup_d = nc.dram_tensor("wsup", [128, 64], F32, kind="ExternalInput").ap()
    wsgt_d = nc.dram_tensor("wsgt", [128, 64], F32, kind="ExternalInput").ap()
    sudp_d = nc.dram_tensor("sudp", [128, 64], F32, kind="ExternalInput").ap()
    hpm_d = nc.dram_tensor("hpm", [128, 128], F32, kind="ExternalInput").ap()
    eye_d = nc.dram_tensor("eye", [128, 128], F32, kind="ExternalInput").ap()
    maskT_d = nc.dram_tensor("maskT", [128, 64], F32, kind="ExternalInput").ap()
    wug_s = nc.dram_tensor("wug_s", [UC, 128, 4096], F32).ap()
    wdn_s = nc.dram_tensor("wdn_s", [32, 128, 4096], F32).ap()

    tap_d = {}
    for name, shape in taps:
        tap_d[name] = nc.dram_tensor("tap_" + name, shape, F32, kind="ExternalOutput").ap()

    es = contextlib.ExitStack()

    ARN = 49472
    AR = es.enter_context(nc.sbuf_tensor("arena", [128, ARN], F32))

    def V(off, *shape):
        n = int(np.prod(shape))
        ap = AR[:, off:off + n]
        if len(shape) > 1:
            pat = " ".join(f"a{i}" for i in range(len(shape)))
            kw = {f"a{i}": s for i, s in enumerate(shape)}
            ap = ap.rearrange(f"p ({pat}) -> p {pat}", **kw)
        return ap

    O_X1 = 0
    O_X1T = 8192
    O_GATED = 16384
    O_WA = 24576
    O_QKV = 41984
    O_GATEH = 45056
    O_QNKN = 46080
    O_TMP = 47104
    O_SM = 48640

    X1 = V(O_X1, NTB, DM)
    X1T = V(O_X1T, FC, G)
    X2 = V(O_X1T, NTB, DM)
    GATED = V(O_GATED, FC, G)
    X2T = V(O_GATED, FC, G)
    WA = [O_WA, O_WA + 8704]
    QKVH = [V(O_QKV, 3, NTB, 128), V(O_QKV + 1536, 3, NTB, 128)]
    GATEH = [V(O_GATEH, G), V(O_GATEH + 512, G)]
    QNKN = V(O_QNKN, 2, G)
    H_OFF = [O_WA + c * 256 for c in range(65)] + [O_TMP + 1792, O_WA + 16640]
    X3_OFF = [O_QKV + s * 256 for s in range(17)] + [O_TMP + 1536, O_WA + 16896]
    H_ = [V(o, TF) for o in H_OFF]
    X3 = [V(o, TF) for o in X3_OFF]
    SQQ = V(O_TMP, G)
    SQK = V(O_TMP + 512, G)
    TMPE = [V(O_TMP, 512), V(O_TMP + 512, 512)]
    SG = [V(O_TMP + 1024, TF), V(O_TMP + 1024 + 256, TF)]
    WSP = 46336

    O = O_SM
    BETA = V(O, 64); O += 64
    INV = [V(O, 8), V(O + 8, 8)]; O += 16
    NRM = V(O, 8); O += 8
    DQ = [V(O, 128), V(O + 128, 128)]; O += 256
    DK = [V(O, 128), V(O + 128, 128)]; O += 256
    SMT = V(O, 128); O += 128
    BNST = V(O, 24); O += 24
    MV = V(O, 2); O += 2
    SD = V(O, 1); O += 1
    SINV = V(O, 1); O += 1
    NEGMS = V(O, 1); O += 1
    BTMP = V(O, 64); O += 64
    DUMW = V(O, 1); O += 1
    assert O <= ARN, O

    c_hpm = es.enter_context(nc.sbuf_tensor("c_hpm", [128, 128], F32))
    c_eye = es.enter_context(nc.sbuf_tensor("c_eye", [128, 128], F32))
    c_maskT = es.enter_context(nc.sbuf_tensor("c_maskT", [128, 64], F32))
    c_bb4 = es.enter_context(nc.sbuf_tensor("c_bb4", [128, 64], F32))
    c_suug = es.enter_context(nc.sbuf_tensor("c_suug", [128, 16], F32))
    c_wsup = es.enter_context(nc.sbuf_tensor("c_wsup", [128, 64], F32))
    c_wsgt = es.enter_context(nc.sbuf_tensor("c_wsgt", [128, 64], F32))
    c_sudp = es.enter_context(nc.sbuf_tensor("c_sudp", [128, 64], F32))
    c_wb = es.enter_context(nc.sbuf_tensor("c_wb", [128, FC, 16], F32))
    c_supc = es.enter_context(nc.sbuf_tensor("c_supc", [128, 64], F32))
    c_s128 = es.enter_context(nc.sbuf_tensor("c_s128", [128, 1], F32))
    c_rsq2048 = es.enter_context(nc.sbuf_tensor("c_rsq2048", [128, 1], F32))
    c_rsq8192 = es.enter_context(nc.sbuf_tensor("c_rsq8192", [128, 1], F32))
    c_eps = es.enter_context(nc.sbuf_tensor("c_eps", [128, 1], F32))
    c_eps12 = es.enter_context(nc.sbuf_tensor("c_eps12", [128, 1], F32))
    c_neg1 = es.enter_context(nc.sbuf_tensor("c_neg1", [128, 1], F32))
    c_bosc = es.enter_context(nc.sbuf_tensor("c_bosc", [128, DM], F32))

    PB = [es.enter_context(nc.psum_tensor(f"pb{i}", [128, 512], F32)) for i in range(8)]

    sems = {}
    for s in ("sq", "aq", "pe", "dv", "ac", "dvs", "gp"):
        sems[s] = es.enter_context(nc.semaphore(s))

    p = Prog(sems)
    bank_tok = [None] * 8

    def dma(eng, out_ap, in_ap, sem="sq"):
        p.emit(eng, lambda e, o=out_ap, i=in_ap: e.dma_start(out=o, in_=i), incs=[(sem, 16)])
        return p.tok(sem)

    for out_ap, in_ap in [
        (c_hpm[:, :], hpm_d), (c_eye[:, :], eye_d), (c_maskT[:, :], maskT_d),
        (c_bb4[:, :], bb4_d), (c_suug[:, :], suug_d), (c_wsup[:, :], wsup_d),
        (c_wsgt[:, :], wsgt_d), (c_sudp[:, :], sudp_d), (c_wb[:, :, :], wb_d),
    ]:
        dma("sync", out_ap, in_ap)
    setup_ld = p.tok("sq")

    p.emit("vector", lambda e: e.memset(c_s128[:, :], 128.0))
    p.emit("vector", lambda e: e.memset(c_rsq2048[:, :], 2048.0))
    p.emit("vector", lambda e: e.memset(c_rsq8192[:, :], 8192.0))
    p.emit("vector", lambda e: e.memset(c_eps[:, :], 1e-5))
    p.emit("vector", lambda e: e.memset(c_eps12[:, :], 1e-12))
    p.emit("vector", lambda e: e.memset(c_neg1[:, :], -1.0), incs=[("dv", 1)])
    p.wait("scalar", "dv")
    p.emit("scalar", lambda e: e.activation(c_s128[:, :], c_s128[:, :], AF.Sqrt))
    p.emit("scalar", lambda e: e.activation(c_rsq2048[:, :], c_rsq2048[:, :], AF.Sqrt))
    p.emit("scalar", lambda e: e.activation(c_rsq8192[:, :], c_rsq8192[:, :], AF.Sqrt),
           incs=[("ac", 1)])
    p.wait("vector", "ac")
    p.emit("vector", lambda e: e.reciprocal(c_rsq2048[:, :], c_rsq2048[:, :]))
    p.emit("vector", lambda e: e.reciprocal(c_rsq8192[:, :], c_rsq8192[:, :]),
           incs=[("dvs", 1)])
    p.wait("vector", "sq", setup_ld[1])
    p.emit("vector", lambda e: e.tensor_tensor(c_supc[:, :], c_wsup[:, :], c_sudp[:, :], ALU.mult))
    p.wait("vector", "dvs")
    p.emit("vector", lambda e: e.tensor_scalar(c_supc[:, :], c_supc[:, :], c_rsq8192[:, 0:1],
                                               None, ALU.mult), incs=[("dv", 1)])

    g_down_mms = None
    g_store = None
    g_upg = None
    g_wscdp_reads = None
    wa_rel = [None, None]
    wf_rel = [None, None]

    def tap(name, src_ap, block=False):
        if name in tap_d and g == 0:
            p.wait("gpsimd", "dv")
            p.wait("gpsimd", "ac")
            p.wait("gpsimd", "pe")
            dma("gpsimd", tap_d[name], src_ap, sem="aq")
            if block:
                p.wait("vector", "aq")
                p.wait("tensor", "aq")

    for g in range(NG):
        p.wait_tok("sync", g_down_mms)
        x_src = x_d[g * G:(g + 1) * G, :].rearrange("(tb q) f -> q tb f", q=128)
        x1_ld = dma("sync", X1[:, :, :], x_src)

        p.wait("tensor", "sq", x1_ld[1])
        first_evac = True
        for tb in range(NTB):
            for icg in range(4):
                b = (tb * 4 + icg) % 2
                p.wait_tok("tensor", bank_tok[b])
                for k in range(4):
                    ic = icg * 4 + k
                    p.emit("tensor",
                           lambda e, tb=tb, ic=ic, b=b, k=k: e.transpose(
                               PB[b][:, k * 128:(k + 1) * 128],
                               X1[:, tb, ic * 128:(ic + 1) * 128], c_eye[:, :]),
                           incs=[("pe", 1)] if k == 3 else ())
                t = p.tok("pe")
                p.wait("vector", "pe", t[1])
                if first_evac:
                    p.wait_tok("vector", g_store)
                    first_evac = False
                p.emit("vector",
                       lambda e, tb=tb, icg=icg, b=b: e.tensor_copy(
                           X1T[:, icg * 4:icg * 4 + 4, tb * 128:(tb + 1) * 128], PB[b][:, :]),
                       incs=[("dv", 1)])
                bank_tok[b] = p.tok("dv")
        x1t_done = p.tok("dv")

        p.wait("tensor", "dv", x1t_done[1])
        p.wait_tok("tensor", bank_tok[2])
        for tb in range(NTB):
            for ic in range(FC):
                p.emit("tensor",
                       lambda e, tb=tb, ic=ic: e.matmul(
                           PB[2][:, tb * 16:(tb + 1) * 16],
                           X1T[:, ic, tb * 128:(tb + 1) * 128], c_wb[:, ic, :],
                           start=(ic == 0), stop=(ic == FC - 1)),
                       incs=[("pe", 1)] if (ic == FC - 1 and tb == NTB - 1) else ())
        p.wait("vector", "pe")
        p.emit("vector", lambda e: e.tensor_tensor(BTMP, PB[2][:, 0:64], c_bb4[:, :], ALU.add),
               incs=[("dv", 1)])
        bank_tok[2] = p.tok("dv")
        p.wait("scalar", "dv")
        p.emit("scalar", lambda e: e.activation(BTMP, BTMP, AF.Exp))
        p.emit("scalar", lambda e: e.activation(BETA, BTMP, AF.Ln, bias=1.0), incs=[("ac", 1)])
        tap("beta", BETA)

        def load_attn(h):
            s = h % 2
            for t_ in (wa_rel[s] or []):
                p.wait_tok("sync", t_)
            qv = V(WA[s], FC, 384)
            ov = V(WA[s] + 6144, FC, 128)
            bv = V(WA[s] + 8192, 384)
            dma("sync", qv[:, :, :], wqkv_d[h])
            dma("sync", ov[:, :, :], wog_d[h])
            ld = dma("sync", bv, bqkv_d[h])
            return ld, qv, ov, bv

        slot_info = {0: load_attn(0), 1: load_attn(1)}
        inv_rdy = [None, None]
        gate_rdy = [None, None]
        qkv_free = [None, None]
        gate_free = [None, None]
        b7_free = None
        out_tok = [None, None]
        sm_bank_tok = [None, None]
        dtp_tok = [None, None]

        def proj_head(h):
            s = h % 2
            ld, qv, ov, bv = slot_info[s]
            p.wait("tensor", "sq", ld[1])
            for tb in range(NTB):
                b = tb % 2
                p.wait_tok("tensor", bank_tok[b])
                for ic in range(FC):
                    p.emit("tensor",
                           lambda e, tb=tb, ic=ic, b=b, qv=qv: e.matmul(
                               PB[b][:, 0:384], X1T[:, ic, tb * 128:(tb + 1) * 128],
                               qv[:, ic, :], start=(ic == 0), stop=(ic == FC - 1)),
                           incs=[("pe", 1)] if ic == FC - 1 else ())
                mt = p.tok("pe")
                p.wait("vector", "pe", mt[1])
                if tb == 0:
                    p.wait_tok("vector", qkv_free[s])
                p.emit("vector",
                       lambda e, tb=tb, b=b, bv=bv, s=s: e.tensor_tensor(
                           QKVH[s][:, :, tb, :], PB[b][:, 0:384],
                           bv.rearrange("q (a b) -> q a b", b=128), ALU.add),
                       incs=[("dv", 1)])
                bank_tok[b] = p.tok("dv")
            qkv_rdy = p.tok("dv")
            p.wait_tok("tensor", bank_tok[2])
            for ic in range(FC):
                p.emit("tensor",
                       lambda e, ic=ic, ov=ov: e.matmul(
                           PB[2][:, :], ov[:, ic, :], X1T[:, ic, :],
                           start=(ic == 0), stop=(ic == FC - 1)),
                       incs=[("pe", 1)] if ic == FC - 1 else ())
            og_t = p.tok("pe")
            wa_rel[s] = [og_t, qkv_rdy]
            p.wait("scalar", "pe", og_t[1])
            p.wait_tok("scalar", gate_free[s])
            p.emit("scalar", lambda e, s=s: e.activation(GATEH[s], PB[2][:, :], AF.Silu),
                   incs=[("ac", 1)])
            bank_tok[2] = p.tok("ac")
            gate_rdy[s] = p.tok("ac")
            p.wait("vector", "dv", qkv_rdy[1])
            p.emit("vector", lambda e, s=s: e.tensor_tensor(
                SQQ, QKVH[s][:, 0, :, :], QKVH[s][:, 0, :, :], ALU.mult))
            p.emit("vector", lambda e, s=s: e.tensor_tensor(
                SQK, QKVH[s][:, 1, :, :], QKVH[s][:, 1, :, :], ALU.mult))
            p.emit("vector", lambda e: e.reduce_sum(
                NRM[:, 0:4], SQQ.rearrange("p (a b) -> p a b", b=128), axis=AX.X))
            p.emit("vector", lambda e: e.reduce_sum(
                NRM[:, 4:8], SQK.rearrange("p (a b) -> p a b", b=128), axis=AX.X),
                incs=[("dv", 1)])
            p.wait("scalar", "dv")
            p.emit("scalar", lambda e, s=s: e.activation(INV[s][:, :], NRM[:, :], AF.Sqrt,
                                                         scale=1.0 / 128.0), incs=[("ac", 1)])
            p.wait("vector", "ac")
            p.emit("vector", lambda e, s=s: e.tensor_scalar(
                INV[s][:, :], INV[s][:, :], c_eps12[:, 0:1], c_s128[:, 0:1], ALU.max, ALU.mult),
                incs=[("dvs", 1)])
            p.wait("vector", "dvs")
            p.emit("vector", lambda e, s=s: e.reciprocal(INV[s][:, :], INV[s][:, :]),
                   incs=[("dvs", 1)])
            inv_rdy[s] = p.tok("dvs")

        def core_head(h):
            nonlocal b7_free
            s = h % 2
            p.wait("vector", "dvs", inv_rdy[s][1])
            for tb in range(NTB):
                b = 3 + tb // 2
                ds = tb % 2
                p.wait_tok("vector", dtp_tok[ds])
                p.emit("vector", lambda e, s=s, tb=tb, ds=ds: e.tensor_scalar(
                    DQ[ds], c_eye[:, :], INV[s][:, tb:tb + 1], None, ALU.mult))
                p.emit("vector", lambda e, s=s, tb=tb, ds=ds: e.tensor_scalar(
                    DK[ds], c_eye[:, :], INV[s][:, 4 + tb:5 + tb], None, ALU.mult),
                    incs=[("dv", 1)])
                dt = p.tok("dv")
                p.wait("tensor", "dv", dt[1])
                if tb % 2 == 0:
                    p.wait_tok("tensor", bank_tok[b])
                p.emit("tensor", lambda e, s=s, tb=tb, b=b, ds=ds: e.matmul(
                    PB[b][:, (tb % 2) * 256:(tb % 2) * 256 + 128],
                    QKVH[s][:, 0, tb, :], DQ[ds], start=True, stop=True))
                p.emit("tensor", lambda e, s=s, tb=tb, b=b, ds=ds: e.matmul(
                    PB[b][:, (tb % 2) * 256 + 128:(tb % 2) * 256 + 256],
                    QKVH[s][:, 1, tb, :], DK[ds], start=True, stop=True),
                    incs=[("pe", 1)])
                dtp_tok[ds] = p.tok("pe")
                if tb % 2 == 1:
                    t = p.tok("pe")
                    p.wait("vector", "pe", t[1])
                    tbb = tb - 1
                    dst = sub_ap(QNKN, tbb * 128, [[128, 2], [512, 2], [1, 128]])
                    src = sub_ap(PB[b][:, :], 0, [[256, 2], [128, 2], [1, 128]])
                    p.emit("vector", lambda e, dst=dst, src=src: e.tensor_copy(dst, src),
                           incs=[("dv", 1)])
                    bank_tok[b] = p.tok("dv")
            qnkn_rdy = p.tok("dv")
            p.wait("tensor", "dv", qnkn_rdy[1])
            for c in range(8):
                off = (c % 2) * 64
                sb = 5 + (c % 2)
                p.wait_tok("tensor", sm_bank_tok[c % 2])
                if c == 0:
                    p.wait_tok("tensor", b7_free)
                p.emit("tensor", lambda e, c=c, off=off, sb=sb: e.matmul(
                    PB[sb][off:off + 64, 0:64],
                    QNKN[:, 1, c * 64:(c + 1) * 64], QNKN[:, 0, c * 64:(c + 1) * 64],
                    start=True, stop=True), incs=[("pe", 1)])
                st_ = p.tok("pe")
                p.wait("vector", "pe", st_[1])
                p.wait_tok("vector", out_tok[c % 2])
                p.emit("vector", lambda e, c=c, off=off, sb=sb, h=h: e.scalar_tensor_tensor(
                    SMT[off:off + 64, off:off + 64], PB[sb][off:off + 64, 0:64],
                    BETA[off:off + 64, (c // 2) * 16 + h:(c // 2) * 16 + h + 1],
                    c_maskT[off:off + 64, :], ALU.mult, ALU.mult), incs=[("dv", 1)])
                sm_bank_tok[c % 2] = p.tok("dv")
                p.wait("tensor", "dv", sm_bank_tok[c % 2][1])
                p.emit("tensor", lambda e, c=c, off=off, s=s: e.matmul(
                    PB[7][:, c * 64:(c + 1) * 64],
                    QKVH[s][off:off + 64, 2, c // 2, :],
                    SMT[off:off + 64, off:off + 64], start=True, stop=True),
                    incs=[("pe", 1)])
                out_tok[c % 2] = p.tok("pe")
            p.wait("vector", "pe", out_tok[1][1])
            p.wait("vector", "ac", gate_rdy[s][1])
            p.emit("vector", lambda e, h=h, s=s: e.tensor_tensor(
                GATED[:, h, :], PB[7][:, :], GATEH[s], ALU.mult), incs=[("dv", 1)])
            b7_free = p.tok("dv")
            qkv_free[s] = p.tok("dv")
            gate_free[s] = p.tok("dv")

        for h in range(NH):
            proj_head(h)
            if h == 0:
                tap("qkv0", QKVH[0][:, :, :, :])
                tap("inv0", INV[0][:, :])
            if h + 2 < NH:
                slot_info[h % 2] = load_attn(h + 2)
            if h >= 1:
                core_head(h - 1)
            if h == 1:
                tap("qnkn0", QNKN[:, :, :])
                tap("smt0", SMT[:, :])
        core_head(NH - 1)
        gated_done = p.tok("dv")
        tap("gated", GATED[:, :, :])

        p.wait_tok("sync", g_wscdp_reads)
        bo_ld = dma("sync", c_bosc[:, :], bo_d)

        def load_wo(ob):
            s = ob % 2
            for t_ in (wa_rel[s] or []):
                p.wait_tok("sync", t_)
            wv = V(WA[s], FC, 512)
            ld = dma("sync", wv[:, :, :], wo_d[ob])
            return ld, wv

        wo_info = {0: load_wo(0), 1: load_wo(1)}
        p.wait("tensor", "dv", gated_done[1])
        ln1_apply = []
        for ob in range(4):
            ld, wv = wo_info[ob % 2]
            p.wait("tensor", "sq", ld[1])
            for tb in range(NTB):
                b = (ob % 2) * 4 + tb
                p.wait_tok("tensor", bank_tok[b])
                for fc in range(FC):
                    p.emit("tensor", lambda e, tb=tb, fc=fc, b=b, wv=wv: e.matmul(
                        PB[b][:, :], GATED[:, fc, tb * 128:(tb + 1) * 128], wv[:, fc, :],
                        start=(fc == 0), stop=(fc == FC - 1)),
                        incs=[("pe", 1)] if fc == FC - 1 else ())
            ot = p.tok("pe")
            wa_rel[ob % 2] = [ot]
            if ob + 2 < 4:
                wo_info[ob % 2] = load_wo(ob + 2)
            p.wait("vector", "pe", ot[1])
            p.wait("vector", "sq", bo_ld[1])
            for tb in range(NTB):
                b = (ob % 2) * 4 + tb
                p.emit("vector", lambda e, tb=tb, ob=ob, b=b: e.tensor_tensor(
                    TMPE[tb % 2], PB[b][:, :], c_bosc[:, ob * 512:(ob + 1) * 512], ALU.add))
                p.emit("vector", lambda e, tb=tb, ob=ob: e.tensor_tensor(
                    X1[:, tb, ob * 512:(ob + 1) * 512], X1[:, tb, ob * 512:(ob + 1) * 512],
                    TMPE[tb % 2], ALU.add), incs=[("dv", 1)])
                bank_tok[b] = p.tok("dv")
        for tb in range(NTB):
            for i in range(4):
                p.emit("vector", lambda e, tb=tb, i=i: e.bn_stats(
                    BNST[:, i * 6:(i + 1) * 6], X1[:, tb, i * 512:(i + 1) * 512]),
                    incs=[("dvs", 1)])
            p.wait("vector", "dvs")
            p.emit("vector", lambda e: e.bn_aggr(MV[:, :], BNST[:, :]), incs=[("dvs", 1)])
            p.wait("vector", "dvs")
            p.emit("vector", lambda e: e.tensor_copy(DUMW, MV[:, 1:2]), incs=[("dv", 1)])
            p.wait("scalar", "dv")
            p.emit("scalar", lambda e: e.activation(SD, MV[:, 1:2], AF.Sqrt, bias=c_eps[:, 0:1]),
                   incs=[("ac", 1)])
            p.wait("vector", "ac")
            p.emit("vector", lambda e: e.reciprocal(SINV, SD), incs=[("dvs", 1)])
            p.wait("vector", "dvs")
            p.emit("vector", lambda e: e.tensor_scalar(
                NEGMS, MV[:, 0:1], SINV[:, 0:1], c_neg1[:, 0:1], ALU.mult, ALU.mult),
                incs=[("dvs", 1)])
            p.wait("vector", "dvs")
            p.emit("vector", lambda e, tb=tb: e.tensor_scalar(
                X2[:, tb, :], X1[:, tb, :], SINV[:, 0:1], NEGMS[:, 0:1], ALU.mult, ALU.add),
                incs=[("dv", 1)])
            ln1_apply.append(p.tok("dv"))
        tap("x2", X2[:, :, :])

        p.wait("sync", "dv")
        wscdp_ld = dma("sync", c_bosc[:, :], wscdp_d)

        for ic in range(FC):
            b = 4 + ic % 2
            p.wait_tok("tensor", bank_tok[b])
            for tb in range(NTB):
                p.wait("tensor", "dv", ln1_apply[tb][1])
                p.emit("tensor", lambda e, ic=ic, tb=tb, b=b: e.transpose(
                    PB[b][:, tb * 128:(tb + 1) * 128], X2[:, tb, ic * 128:(ic + 1) * 128],
                    c_eye[:, :]), incs=[("pe", 1)] if tb == NTB - 1 else ())
            t = p.tok("pe")
            p.wait("vector", "pe", t[1])
            p.emit("vector", lambda e, ic=ic, b=b: e.tensor_scalar(
                X2T[:, ic, :], PB[b][:, :], c_suug[:, ic:ic + 1], c_rsq2048[:, 0:1],
                ALU.mult, ALU.mult), incs=[("dv", 1)])
            bank_tok[b] = p.tok("dv")
        tap("x2t", X2T[:, :, :])

        h_free = None
        down_last = None

        def x3h(sl):
            return V(X3_OFF[sl], 128).bitcast(BF16)

        def x3l(sl):
            return V(X3_OFF[sl] + 128, 128).bitcast(BF16)

        def hh(sl):
            return V(H_OFF[sl], 128).bitcast(BF16)

        def hl(sl):
            return V(H_OFF[sl] + 128, 128).bitcast(BF16)

        wf_store = [None, None]

        def bfly_dual(slotmap, tiles, strides, nlog, sp0, sp1, ndve):
            sp_d, sp_g = sp0, sp1
            for si, stride in enumerate(strides):
                p.wait("gpsimd", "dv")
                if si > 0:
                    p.wait("vector", "gp")
                pairs = [(base + j, base + j + stride)
                         for base in range(0, nlog, 2 * stride) for j in range(stride)]
                nd = min(ndve, len(pairs))
                for pi, (a_l, b_l) in enumerate(pairs):
                    dve = pi < nd
                    a_s, b_s = slotmap[a_l], slotmap[b_l]
                    sp = sp_d if dve else sp_g
                    last_d = dve and (pi == nd - 1 or pi == len(pairs) - 1)
                    last_g = (not dve) and pi == len(pairs) - 1
                    if dve:
                        p.emit("vector", lambda e, a=a_s, b=b_s, sp=sp: e.tensor_tensor(
                            tiles[sp], tiles[a], tiles[b], ALU.add))
                        p.emit("vector", lambda e, a=a_s, b=b_s: e.tensor_tensor(
                            tiles[a], tiles[a], tiles[b], ALU.subtract),
                            incs=[("dv", 1)] if pi == nd - 1 else ())
                    else:
                        p.emit("gpsimd", lambda e, a=a_s, b=b_s, sp=sp: e.tensor_add(
                            tiles[sp], tiles[a], tiles[b]))
                        p.emit("gpsimd", lambda e, a=a_s, b=b_s: e.tensor_sub(
                            tiles[a], tiles[a], tiles[b]),
                            incs=[("gp", 1)] if pi == len(pairs) - 1 else ())
                    slotmap[a_l], slotmap[b_l] = sp, a_s
                    if dve:
                        sp_d = b_s
                    else:
                        sp_g = b_s
                if len(pairs) <= nd:
                    p.emit("gpsimd", lambda e: e.tensor_copy(tiles[sp_g][:, 0:1],
                                                             tiles[sp_g][:, 0:1]),
                           incs=[("gp", 1)])
            p.wait("vector", "gp")
            p.wait("scalar", "gp")
            return sp_d, sp_g

        def split_load(s_, hbm_ap, nblk, blkf, scratch_ap, spbase):
            wb_ = O_X1 + s_ * 4096
            dma("sync", V(wb_, 4096), hbm_ap)
            ldt = p.tok("sq")
            p.wait("scalar", "sq", ldt[1])
            dests = [spbase] + [wb_ + k * blkf for k in range(nblk - 1)]
            subtok = {}
            for k in range(nblk):
                src = V(wb_ + k * blkf, blkf)
                hi = V(dests[k], blkf // 2).bitcast(BF16)
                lo = V(dests[k] + blkf // 2, blkf // 2).bitcast(BF16)
                if k >= 1:
                    p.wait_tok("scalar", subtok[k - 1])
                p.emit("scalar", lambda e, hi=hi, src=src: e.copy(hi, src), incs=[("ac", 1)])
                ct = p.tok("ac")
                p.wait("vector", "ac", ct[1])
                p.emit("vector", lambda e, lo=lo, src=src, hi=hi: e.tensor_tensor(
                    lo, src, hi, ALU.subtract), incs=[("dv", 1)])
                subtok[k] = p.tok("dv")
            pairs_tok = p.tok("dv")
            p.wait("scalar", "dv", pairs_tok[1])
            dma("scalar", scratch_ap[:, 0:blkf], V(spbase, blkf), sem="aq")
            dma("scalar", scratch_ap[:, blkf:nblk * blkf], V(wb_, (nblk - 1) * blkf),
                sem="aq")
            wf_store[s_] = p.tok("aq")
            return pairs_tok, dests

        for tf in range(NTF):
            ffn_first = (g == 0 and tf == 0)
            x3map = list(range(16)); x3spare = 16
            pend = None
            for ic in range(FC + 1):
                if ic < FC:
                    b = 4 + ic % 2
                    p.wait_tok("tensor", bank_tok[b])
                    p.emit("tensor", lambda e, ic=ic, tf=tf, b=b: e.matmul(
                        PB[b][:, 0:TF], c_hpm[:, :], X2T[:, ic, tf * TF:(tf + 1) * TF],
                        start=True, stop=True), incs=[("pe", 1)])
                    t = p.tok("pe")
                if pend is not None:
                    pic, pb_ = pend
                    p.wait("vector", "pe", t[1])
                    if pic == 0:
                        p.wait_tok("vector", g_upg)
                    p.emit("vector", lambda e, pic=pic, pb_=pb_: e.tensor_copy(
                        X3[pic], PB[pb_][:, 0:TF]), incs=[("dv", 1)])
                    bank_tok[pb_] = p.tok("dv")
                if ic < FC:
                    pend = (ic, b)
            x3sp_d, x3sp_g = bfly_dual(x3map, X3, (8, 4, 2, 1), 16, x3spare, 18, 5)
            p.emit("vector", lambda e: e.tensor_copy(DUMW, c_s128[:, 0:1]),
                   incs=[("dv", 1)])
            p.wait("scalar", "dv")
            p.wait_tok("scalar", g_upg)
            spq = [x3sp_d, 17]
            subtok = {}
            for lc in range(16):
                cs = x3map[lc]; sp = spq.pop(0)
                if lc >= 2:
                    p.wait_tok("scalar", subtok[lc - 2])
                p.emit("scalar", lambda e, cs=cs, sp=sp: e.copy(x3h(sp), X3[cs]),
                       incs=[("ac", 1)])
                ct = p.tok("ac")
                p.wait("vector", "ac", ct[1])
                p.emit("vector", lambda e, cs=cs, sp=sp: e.tensor_tensor(
                    x3l(sp), X3[cs], x3h(sp), ALU.subtract), incs=[("dv", 1)])
                subtok[lc] = p.tok("dv")
                x3map[lc] = sp
                spq.append(cs)
            x3_done = p.tok("dv")
            if "x3p" in tap_d and g == 0 and tf == 0:
                p.wait("gpsimd", "dv")
                for lc_ in range(16):
                    dma("gpsimd", tap_d["x3p"][lc_ * 128:(lc_ + 1) * 128, :],
                        V(X3_OFF[x3map[lc_]], 256), sem="aq")
                p.wait("vector", "aq")
                p.wait("scalar", "aq")
                p.wait("tensor", "aq")

            def load_wug(uc):
                s_ = uc % 2
                for t_ in (wf_rel[s_] or []):
                    p.wait_tok("sync", t_)
                wb_ = O_X1 + s_ * 4096
                if not ffn_first:
                    ld = dma("sync", V(wb_, 4096), wug_s[uc])
                    return ld, [wb_ + k * 256 for k in range(16)]
                return split_load(s_, wug_d[uc], 16, 256, wug_s[uc], WSP + s_ * 256)

            wug_info = {0: load_wug(0), 1: load_wug(1)}
            sg_tok = [None, None]
            h_rdy = []
            p.wait("tensor", "dv", x3_done[1])
            hm = list(range(64)); hspare = 64
            for uc in range(UC):
                ld, poff = wug_info[uc % 2]
                b = 6 + uc % 2

                def uph(ic, poff=poff):
                    return V(poff[ic], 64).bitcast(BF16)

                def gth(ic, poff=poff):
                    return V(poff[ic] + 64, 64).bitcast(BF16)

                def upl(ic, poff=poff):
                    return V(poff[ic] + 128, 64).bitcast(BF16)

                def gtl(ic, poff=poff):
                    return V(poff[ic] + 192, 64).bitcast(BF16)

                p.wait_tok("tensor", ld)
                p.wait_tok("tensor", bank_tok[b])
                if uc == 0:
                    p.wait_tok("tensor", h_free)
                for ic in range(FC):
                    xh_, xl_ = x3h(x3map[ic]), x3l(x3map[ic])
                    p.emit("tensor", lambda e, w=uph(ic), x=xh_, b=b, st=(ic == 0): e.matmul(
                        PB[b][:, 0:TF], w, x, start=st, stop=False))
                    p.emit("tensor", lambda e, w=uph(ic), x=xl_, b=b: e.matmul(
                        PB[b][:, 0:TF], w, x, start=False, stop=False))
                    p.emit("tensor", lambda e, w=upl(ic), x=xh_, b=b,
                           sp=(ic == FC - 1): e.matmul(
                        PB[b][:, 0:TF], w, x, start=False, stop=sp))
                for ic in range(FC):
                    xh_, xl_ = x3h(x3map[ic]), x3l(x3map[ic])
                    p.emit("tensor", lambda e, w=gth(ic), x=xh_, b=b, st=(ic == 0): e.matmul(
                        PB[b][:, TF:2 * TF], w, x, start=st, stop=False))
                    p.emit("tensor", lambda e, w=gth(ic), x=xl_, b=b: e.matmul(
                        PB[b][:, TF:2 * TF], w, x, start=False, stop=False))
                    p.emit("tensor", lambda e, w=gtl(ic), x=xh_, b=b,
                           sp=(ic == FC - 1): e.matmul(
                        PB[b][:, TF:2 * TF], w, x, start=False, stop=sp),
                        incs=[("pe", 1)] if ic == FC - 1 else ())
                mt = p.tok("pe")
                wf_rel[uc % 2] = [mt] + ([wf_store[uc % 2]] if wf_store[uc % 2] else [])
                wf_store[uc % 2] = None
                if uc + 2 < UC:
                    wug_info[uc % 2] = load_wug(uc + 2)
                p.wait("scalar", "pe", mt[1])
                p.wait_tok("scalar", sg_tok[uc % 2])
                p.emit("scalar", lambda e, uc=uc, b=b: e.activation(
                    SG[uc % 2], PB[b][:, TF:2 * TF], AF.Silu, scale=c_wsgt[:, uc:uc + 1]),
                    incs=[("ac", 1)])
                sg_rdy = p.tok("ac")
                p.wait("vector", "ac", sg_rdy[1])
                p.wait("vector", "pe", mt[1])
                p.emit("vector", lambda e, uc=uc, b=b: e.scalar_tensor_tensor(
                    H_[uc], PB[b][:, 0:TF], c_supc[:, uc:uc + 1], SG[uc % 2],
                    ALU.mult, ALU.mult), incs=[("dv", 1)])
                bank_tok[b] = p.tok("dv")
                sg_tok[uc % 2] = p.tok("dv")
                h_rdy.append(p.tok("dv"))
            g_upg = p.tok("pe")

            for uc in range(UC):
                b = 4 + uc % 2
                p.wait_tok("tensor", bank_tok[b])
                p.wait_tok("tensor", h_rdy[uc])
                p.emit("tensor", lambda e, uc=uc, b=b: e.matmul(
                    PB[b][:, 0:TF], c_hpm[:, :], H_[uc], start=True, stop=True),
                    incs=[("pe", 1)])
                t = p.tok("pe")
                p.wait("vector", "pe", t[1])
                p.emit("vector", lambda e, uc=uc, b=b: e.tensor_copy(H_[uc], PB[b][:, 0:TF]),
                       incs=[("dv", 1)])
                bank_tok[b] = p.tok("dv")
            hsp_d, hsp_g = bfly_dual(hm, H_, (32, 16, 8, 4, 2, 1), 64, hspare, 66, 22)
            p.emit("vector", lambda e: e.tensor_copy(DUMW, c_s128[:, 0:1]),
                   incs=[("dv", 1)])
            down_fp32 = (g == 0 and tf == 0)
            wdn_first = (g == 0 and tf == 1)
            p.wait("scalar", "dv")
            spq2 = [hsp_d, 65]
            subtok2 = {}
            for lc in (range(0) if down_fp32 else range(64)):
                cs = hm[lc]; sp = spq2.pop(0)
                if lc >= 2:
                    p.wait_tok("scalar", subtok2[lc - 2])
                p.emit("scalar", lambda e, cs=cs, sp=sp: e.copy(hh(sp), H_[cs]),
                       incs=[("ac", 1)])
                ct = p.tok("ac")
                p.wait("vector", "ac", ct[1])
                p.emit("vector", lambda e, cs=cs, sp=sp: e.tensor_tensor(
                    hl(sp), H_[cs], hh(sp), ALU.subtract), incs=[("dv", 1)])
                subtok2[lc] = p.tok("dv")
                hm[lc] = sp
                spq2.append(cs)
            h_done = p.tok("dv")

            def load_wdn(idx):
                s_ = idx % 2
                for t_ in (wf_rel[s_] or []):
                    p.wait_tok("sync", t_)
                wb_ = O_X1 + s_ * 4096
                if down_fp32:
                    ld = dma("sync", V(wb_, 4096), wdn_d[idx // 8, idx % 8])
                    return ld, [wb_ + k * 512 for k in range(8)]
                if not wdn_first:
                    ld = dma("sync", V(wb_, 4096), wdn_s[idx])
                    return ld, [wb_ + k * 512 for k in range(8)]
                return split_load(s_, wdn_d[idx // 8, idx % 8], 8, 512, wdn_s[idx],
                                  O_QKV + s_ * 512)

            if "hpre" in tap_d and g == 0 and tf == 0:
                p.wait("gpsimd", "dv")
                for uc_ in range(UC):
                    dma("gpsimd", tap_d["hpre"][uc_ * 128:(uc_ + 1) * 128, :], H_[hm[uc_]],
                        sem="aq")
                p.wait("vector", "aq")
                p.wait("scalar", "aq")
                p.wait("tensor", "aq")
            wdn_info = {0: load_wdn(0), 1: load_wdn(1)}
            p.wait("tensor", "dv", h_done[1])
            for ob in range(4):
                for tb in range(2):
                    p.wait_tok("tensor", bank_tok[tb * 2 + (ob % 2)])
                for wave in range(8):
                    idx = ob * 8 + wave
                    ld, poff = wdn_info[idx % 2]

                    def dnh(k, poff=poff):
                        return V(poff[k], 256).bitcast(BF16)

                    def dnl(k, poff=poff):
                        return V(poff[k] + 256, 256).bitcast(BF16)

                    p.wait_tok("tensor", ld)
                    for tb in range(2):
                        for k in range(8):
                            uc = wave * 8 + k
                            b = tb * 2 + (ob % 2)
                            if down_fp32:
                                p.emit("tensor", lambda e, b=b,
                                       lh=H_[hm[uc]][:, tb * 128:(tb + 1) * 128],
                                       r=V(poff[k], 512), st=(uc == 0),
                                       sp=(uc == UC - 1): e.matmul(
                                    PB[b][:, :], lh, r, start=st, stop=sp),
                                    incs=[("pe", 1)] if (k == 7 and tb == 1) else ())
                                continue
                            lhH = hh(hm[uc])[:, tb * 128:(tb + 1) * 128]
                            lhL = hl(hm[uc])[:, tb * 128:(tb + 1) * 128]
                            p.emit("tensor", lambda e, lh=lhH, r=dnh(k), b=b,
                                   st=(uc == 0): e.matmul(
                                PB[b][:, :], lh, r, start=st, stop=False))
                            p.emit("tensor", lambda e, lh=lhH, r=dnl(k), b=b: e.matmul(
                                PB[b][:, :], lh, r, start=False, stop=False))
                            p.emit("tensor", lambda e, lh=lhL, r=dnh(k), b=b,
                                   sp=(uc == UC - 1): e.matmul(
                                PB[b][:, :], lh, r, start=False, stop=sp),
                                incs=[("pe", 1)] if (k == 7 and tb == 1) else ())
                    wf_rel[idx % 2] = [p.tok("pe")] + (
                        [wf_store[idx % 2]] if wf_store[idx % 2] else [])
                    wf_store[idx % 2] = None
                    if idx + 2 < 32:
                        wdn_info[idx % 2] = load_wdn(idx + 2)
                dt = p.tok("pe")
                p.wait("vector", "pe", dt[1])
                p.wait("vector", "sq", wscdp_ld[1])
                for tb in range(2):
                    b = tb * 2 + (ob % 2)
                    p.emit("vector", lambda e, tb=tb, ob=ob, b=b: e.tensor_tensor(
                        TMPE[tb], PB[b][:, :], c_bosc[:, ob * 512:(ob + 1) * 512], ALU.mult))
                    p.emit("vector", lambda e, tb=tb, ob=ob, tf=tf: e.tensor_tensor(
                        X2[:, tf * 2 + tb, ob * 512:(ob + 1) * 512],
                        X2[:, tf * 2 + tb, ob * 512:(ob + 1) * 512], TMPE[tb], ALU.add),
                        incs=[("dv", 1)])
                    bank_tok[b] = p.tok("dv")
            down_last = p.tok("pe")
            h_free = down_last
            g_wscdp_reads = p.tok("dv")
            if "r2" in tap_d and g == 0 and tf == 0:
                p.wait("gpsimd", "dv")
                dma("gpsimd", tap_d["r2"], sub_ap(X2, 0, [[2048, 2], [1, 2048]]), sem="aq")
                p.wait("vector", "aq")
                p.wait("scalar", "aq")

            for tb in range(2):
                for i in range(4):
                    p.emit("vector", lambda e, tb=tb, tf=tf, i=i: e.bn_stats(
                        BNST[:, i * 6:(i + 1) * 6],
                        X2[:, tf * 2 + tb, i * 512:(i + 1) * 512]), incs=[("dvs", 1)])
                p.wait("vector", "dvs")
                p.emit("vector", lambda e: e.bn_aggr(MV[:, :], BNST[:, :]), incs=[("dvs", 1)])
                p.wait("vector", "dvs")
                p.emit("vector", lambda e: e.tensor_copy(DUMW, MV[:, 1:2]), incs=[("dv", 1)])
                p.wait("scalar", "dv")
                p.emit("scalar", lambda e: e.activation(SD, MV[:, 1:2], AF.Sqrt,
                                                        bias=c_eps[:, 0:1]), incs=[("ac", 1)])
                p.wait("vector", "ac")
                p.emit("vector", lambda e: e.reciprocal(SINV, SD), incs=[("dvs", 1)])
                p.wait("vector", "dvs")
                p.emit("vector", lambda e: e.tensor_scalar(
                    NEGMS, MV[:, 0:1], SINV[:, 0:1], c_neg1[:, 0:1], ALU.mult, ALU.mult),
                    incs=[("dvs", 1)])
                p.wait("vector", "dvs")
                p.emit("vector", lambda e, tb=tb, tf=tf: e.tensor_scalar(
                    X2[:, tf * 2 + tb, :], X2[:, tf * 2 + tb, :], SINV[:, 0:1],
                    NEGMS[:, 0:1], ALU.mult, ALU.add), incs=[("dv", 1)])
                st_t = p.tok("dv")
                p.wait("scalar", "dv", st_t[1])
                row0 = g * G + tf * TF + tb * 128
                dma("scalar", y_d[row0:row0 + 128, :], X2[:, tf * 2 + tb, :], sem="aq")
        g_down_mms = down_last
        g_store = p.tok("aq")

    p.wait("gpsimd", "aq")
    p.emit("gpsimd", lambda e: e.nop())

    with nc.Block() as block:
        @block.sync
        def _(e):
            for f in p.ops["sync"]:
                f(e)

        @block.tensor
        def _(e):
            for f in p.ops["tensor"]:
                f(e)

        @block.vector
        def _(e):
            for f in p.ops["vector"]:
                f(e)

        @block.scalar
        def _(e):
            for f in p.ops["scalar"]:
                f(e)

        @block.gpsimd
        def _(e):
            for f in p.ops["gpsimd"]:
                f(e)

    es.close()
    return nc


def _prep_inputs(kw):
    f32 = np.float32
    hs = np.ascontiguousarray(np.asarray(kw["L0_attn_hidden_states"], f32).reshape(T * NCORE, DM))

    parts = []
    for nm in ("L0_attn_q_w", "L0_attn_k_w", "L0_attn_v_w"):
        WT = np.asarray(kw[nm], f32).T
        parts.append(WT.reshape(FC, 128, NH, 128).transpose(2, 1, 0, 3))
    wqkv = np.ascontiguousarray(np.concatenate(parts, axis=3))
    bq = []
    for h in range(NH):
        bq.append(np.concatenate([np.asarray(kw["L0_attn_q_b"], f32)[h * 128:(h + 1) * 128],
                                  np.asarray(kw["L0_attn_k_b"], f32)[h * 128:(h + 1) * 128],
                                  np.asarray(kw["L0_attn_v_b"], f32)[h * 128:(h + 1) * 128]]))
    bqkv = np.ascontiguousarray(np.broadcast_to(np.stack(bq)[:, None, :], (NH, 128, 384)))
    wog = np.ascontiguousarray(np.asarray(kw["L0_attn_og_w"], f32).T
                               .reshape(FC, 128, NH, 128).transpose(2, 1, 0, 3))
    wo = np.ascontiguousarray(np.asarray(kw["L0_attn_o_w"], f32).T
                              .reshape(FC, 128, 4, 512).transpose(2, 1, 0, 3))
    wb = np.ascontiguousarray(np.asarray(kw["L0_attn_b_w"], f32).T.reshape(FC, 128, 16)
                              .transpose(1, 0, 2))
    up = np.asarray(kw["L0_ffn_W_up"], f32).reshape(UC, 128, FC, 128).transpose(0, 3, 2, 1)
    gt = np.asarray(kw["L0_ffn_W_gate"], f32).reshape(UC, 128, FC, 128).transpose(0, 3, 2, 1)
    wug = np.ascontiguousarray(np.stack([up, gt], axis=3))
    wdn = np.ascontiguousarray(np.asarray(kw["L0_ffn_W_down"], f32).T
                               .reshape(8, 8, 128, 4, 512).transpose(3, 0, 2, 1, 4))
    bo = np.ascontiguousarray(np.broadcast_to(np.asarray(kw["L0_attn_o_b"], f32), (128, DM)))
    wscdp = np.ascontiguousarray(np.broadcast_to(np.asarray(kw["L0_ffn_Wscale_dp"], f32),
                                                 (128, DM)))
    bb4 = np.ascontiguousarray(np.broadcast_to(np.tile(np.asarray(kw["L0_attn_b_b"], f32), 4),
                                               (128, 64)))
    suug = np.ascontiguousarray(np.asarray(kw["L0_ffn_SU_ug"], f32).reshape(16, 128).T)
    wsup = np.ascontiguousarray(np.asarray(kw["L0_ffn_Wscale_ug"], f32)[:IN].reshape(64, 128).T)
    wsgt = np.ascontiguousarray(np.asarray(kw["L0_ffn_Wscale_ug"], f32)[IN:].reshape(64, 128).T)
    sudp = np.ascontiguousarray(np.asarray(kw["L0_ffn_SU_dp"], f32).reshape(64, 128).T)
    hpm = _hadamard128()
    eye = np.eye(128, dtype=f32)
    maskT = np.ascontiguousarray(np.tile(np.triu(np.ones((64, 64), f32)), (2, 1)))

    shared = dict(wqkv=wqkv, bqkv=bqkv, wog=wog, wo=wo, wb=wb, wug=wug, wdn=wdn,
                  bo=bo, wscdp=wscdp, bb4=bb4, suug=suug, wsup=wsup, wsgt=wsgt,
                  sudp=sudp, hpm=hpm, eye=eye, maskT=maskT)
    in_maps = []
    for c in range(NCORE):
        m = dict(shared)
        m["x"] = np.ascontiguousarray(hs[c * T:(c + 1) * T])
        in_maps.append(m)
    return in_maps


def kernel(**kw):
    in_maps = _prep_inputs(kw)
    nc = build_nc()
    res = run_bass_kernel_spmd(nc, in_maps, core_ids=list(range(NCORE)))
    y = np.concatenate([np.asarray(r["y"], np.float32) for r in res.results], axis=0)
    return y.reshape(2, 4096, DM)


# version 16
# speedup vs baseline: 1.0839x; 1.0839x over previous
import contextlib
import numpy as np

import concourse.bass as bass
import concourse.mybir as mybir
from concourse.bass_utils import run_bass_kernel_spmd
from concourse.alu_op_type import AluOpType as ALU

F32 = mybir.dt.float32
BF16 = mybir.dt.bfloat16
AF = mybir.ActivationFunctionType
AX = mybir.AxisListType

NCORE = 8
T = 1024
NG = 2
G = 512
NTB = 4
NH = 16
DM = 2048
FC = 16
IN = 8192
UC = 64
TF = 256
NTF = 2

ENGS = ("sync", "tensor", "vector", "scalar", "gpsimd")


class Prog:

    def __init__(self, sems):
        self.ops = {e: [] for e in ENGS}
        self.sems = sems
        self.cnt = {k: 0 for k in sems}

    def emit(self, eng, fn, incs=()):
        caps = [(self.sems[s], n) for s, n in incs]

        def run(e, fn=fn, caps=caps):
            ins = fn(e)
            for h, n in caps:
                ins = ins.then_inc(h, n)

        self.ops[eng].append(run)
        for s, n in incs:
            self.cnt[s] += n

    def tok(self, sem):
        return (sem, self.cnt[sem])

    def wait(self, eng, sem, val=None):
        v = self.cnt[sem] if val is None else val
        if v <= 0:
            return
        h = self.sems[sem]
        self.ops[eng].append(lambda e, h=h, v=v: e.wait_ge(h, v))

    def wait_tok(self, eng, tok):
        if tok is not None:
            self.wait(eng, tok[0], tok[1])


def _hadamard128():
    i = np.arange(128)
    a = (i[:, None] & i[None, :])
    pc = np.zeros((128, 128), dtype=np.int64)
    for b in range(7):
        pc += (a >> b) & 1
    return np.where(pc % 2 == 0, 1.0, -1.0).astype(np.float32)


def sub_ap(view, extra_off, dims):
    return bass.AP(view.tensor, view.offset + extra_off, [list(view.ap[0])] + [list(d) for d in dims])


def build_nc(taps=()):
    nc = bass.Bass("TRN2", target_bir_lowering=False, detect_race_conditions=False)

    x_d = nc.dram_tensor("x", [T, DM], F32, kind="ExternalInput").ap()
    y_d = nc.dram_tensor("y", [T, DM], F32, kind="ExternalOutput").ap()
    wqkv_d = nc.dram_tensor("wqkv", [NH, 128, FC, 384], F32, kind="ExternalInput").ap()
    bqkv_d = nc.dram_tensor("bqkv", [NH, 128, 384], F32, kind="ExternalInput").ap()
    wog_d = nc.dram_tensor("wog", [NH, 128, FC, 128], F32, kind="ExternalInput").ap()
    wo_d = nc.dram_tensor("wo", [4, 128, FC, 512], F32, kind="ExternalInput").ap()
    wb_d = nc.dram_tensor("wb", [128, FC, 16], F32, kind="ExternalInput").ap()
    wug_d = nc.dram_tensor("wug", [UC, 128, FC, 2, 128], F32, kind="ExternalInput").ap()
    wdn_d = nc.dram_tensor("wdn", [4, 8, 128, 8, 512], F32, kind="ExternalInput").ap()
    bo_d = nc.dram_tensor("bo", [128, DM], F32, kind="ExternalInput").ap()
    wscdp_d = nc.dram_tensor("wscdp", [128, DM], F32, kind="ExternalInput").ap()
    bb4_d = nc.dram_tensor("bb4", [128, 64], F32, kind="ExternalInput").ap()
    suug_d = nc.dram_tensor("suug", [128, 16], F32, kind="ExternalInput").ap()
    wsup_d = nc.dram_tensor("wsup", [128, 64], F32, kind="ExternalInput").ap()
    wsgt_d = nc.dram_tensor("wsgt", [128, 64], F32, kind="ExternalInput").ap()
    sudp_d = nc.dram_tensor("sudp", [128, 64], F32, kind="ExternalInput").ap()
    hpm_d = nc.dram_tensor("hpm", [128, 128], F32, kind="ExternalInput").ap()
    eye_d = nc.dram_tensor("eye", [128, 128], F32, kind="ExternalInput").ap()
    maskT_d = nc.dram_tensor("maskT", [128, 64], F32, kind="ExternalInput").ap()
    wug_s = nc.dram_tensor("wug_s", [UC, 128, 4096], F32).ap()
    wdn_s = nc.dram_tensor("wdn_s", [32, 128, 4096], F32).ap()

    tap_d = {}
    for name, shape in taps:
        tap_d[name] = nc.dram_tensor("tap_" + name, shape, F32, kind="ExternalOutput").ap()

    es = contextlib.ExitStack()

    ARN = 49472
    AR = es.enter_context(nc.sbuf_tensor("arena", [128, ARN], F32))

    def V(off, *shape):
        n = int(np.prod(shape))
        ap = AR[:, off:off + n]
        if len(shape) > 1:
            pat = " ".join(f"a{i}" for i in range(len(shape)))
            kw = {f"a{i}": s for i, s in enumerate(shape)}
            ap = ap.rearrange(f"p ({pat}) -> p {pat}", **kw)
        return ap

    O_X1 = 0
    O_X1T = 8192
    O_GATED = 16384
    O_WA = 24576
    O_QKV = 41984
    O_GATEH = 45056
    O_QNKN = 46080
    O_TMP = 47104
    O_SM = 48640

    X1 = V(O_X1, NTB, DM)
    X1T = V(O_X1T, FC, G)
    X2 = V(O_X1T, NTB, DM)
    GATED = V(O_GATED, FC, G)
    X2T = V(O_GATED, FC, G)
    WA = [O_WA, O_WA + 8704]
    QKVH = [V(O_QKV, 3, NTB, 128), V(O_QKV + 1536, 3, NTB, 128)]
    GATEH = [V(O_GATEH, G), V(O_GATEH + 512, G)]
    QNKN = V(O_QNKN, 2, G)
    H_OFF = [O_WA + c * 256 for c in range(65)] + [O_TMP + 1792, O_WA + 16640]
    X3_OFF = [O_QKV + s * 256 for s in range(17)] + [O_TMP + 1536, O_WA + 16896]
    H_ = [V(o, TF) for o in H_OFF]
    X3 = [V(o, TF) for o in X3_OFF]
    SQQ = V(O_TMP, G)
    SQK = V(O_TMP + 512, G)
    TMPE = [V(O_TMP, 512), V(O_TMP + 512, 512)]
    SG = [V(O_TMP + 1024, TF), V(O_TMP + 1024 + 256, TF)]
    WSP = 46336

    O = O_SM
    BETA = V(O, 64); O += 64
    INV = [V(O, 8), V(O + 8, 8)]; O += 16
    NRM = V(O, 8); O += 8
    DQ = [V(O, 128), V(O + 128, 128)]; O += 256
    DK = [V(O, 128), V(O + 128, 128)]; O += 256
    SMT = V(O, 128); O += 128
    BNST = V(O, 24); O += 24
    MV = V(O, 2); O += 2
    SD = V(O, 1); O += 1
    SINV = V(O, 1); O += 1
    NEGMS = V(O, 1); O += 1
    BTMP = V(O, 64); O += 64
    DUMW = V(O, 1); O += 1
    assert O <= ARN, O

    c_hpm = es.enter_context(nc.sbuf_tensor("c_hpm", [128, 128], F32))
    c_eye = es.enter_context(nc.sbuf_tensor("c_eye", [128, 128], F32))
    c_maskT = es.enter_context(nc.sbuf_tensor("c_maskT", [128, 64], F32))
    c_bb4 = es.enter_context(nc.sbuf_tensor("c_bb4", [128, 64], F32))
    c_suug = es.enter_context(nc.sbuf_tensor("c_suug", [128, 16], F32))
    c_wsup = es.enter_context(nc.sbuf_tensor("c_wsup", [128, 64], F32))
    c_wsgt = es.enter_context(nc.sbuf_tensor("c_wsgt", [128, 64], F32))
    c_sudp = es.enter_context(nc.sbuf_tensor("c_sudp", [128, 64], F32))
    c_wb = es.enter_context(nc.sbuf_tensor("c_wb", [128, FC, 16], F32))
    c_supc = es.enter_context(nc.sbuf_tensor("c_supc", [128, 64], F32))
    c_s128 = es.enter_context(nc.sbuf_tensor("c_s128", [128, 1], F32))
    c_rsq2048 = es.enter_context(nc.sbuf_tensor("c_rsq2048", [128, 1], F32))
    c_rsq8192 = es.enter_context(nc.sbuf_tensor("c_rsq8192", [128, 1], F32))
    c_eps = es.enter_context(nc.sbuf_tensor("c_eps", [128, 1], F32))
    c_eps12 = es.enter_context(nc.sbuf_tensor("c_eps12", [128, 1], F32))
    c_neg1 = es.enter_context(nc.sbuf_tensor("c_neg1", [128, 1], F32))
    c_bosc = es.enter_context(nc.sbuf_tensor("c_bosc", [128, DM], F32))

    PB = [es.enter_context(nc.psum_tensor(f"pb{i}", [128, 512], F32)) for i in range(8)]

    sems = {}
    for s in ("sq", "aq", "pe", "dv", "ac", "dvs", "gp"):
        sems[s] = es.enter_context(nc.semaphore(s))

    p = Prog(sems)
    bank_tok = [None] * 8

    def dma(eng, out_ap, in_ap, sem="sq"):
        p.emit(eng, lambda e, o=out_ap, i=in_ap: e.dma_start(out=o, in_=i), incs=[(sem, 16)])
        return p.tok(sem)

    for out_ap, in_ap in [
        (c_hpm[:, :], hpm_d), (c_eye[:, :], eye_d), (c_maskT[:, :], maskT_d),
        (c_bb4[:, :], bb4_d), (c_suug[:, :], suug_d), (c_wsup[:, :], wsup_d),
        (c_wsgt[:, :], wsgt_d), (c_sudp[:, :], sudp_d), (c_wb[:, :, :], wb_d),
    ]:
        dma("sync", out_ap, in_ap)
    setup_ld = p.tok("sq")

    p.emit("vector", lambda e: e.memset(c_s128[:, :], 128.0))
    p.emit("vector", lambda e: e.memset(c_rsq2048[:, :], 2048.0))
    p.emit("vector", lambda e: e.memset(c_rsq8192[:, :], 8192.0))
    p.emit("vector", lambda e: e.memset(c_eps[:, :], 1e-5))
    p.emit("vector", lambda e: e.memset(c_eps12[:, :], 1e-12))
    p.emit("vector", lambda e: e.memset(c_neg1[:, :], -1.0), incs=[("dv", 1)])
    p.wait("scalar", "dv")
    p.emit("scalar", lambda e: e.activation(c_s128[:, :], c_s128[:, :], AF.Sqrt))
    p.emit("scalar", lambda e: e.activation(c_rsq2048[:, :], c_rsq2048[:, :], AF.Sqrt))
    p.emit("scalar", lambda e: e.activation(c_rsq8192[:, :], c_rsq8192[:, :], AF.Sqrt),
           incs=[("ac", 1)])
    p.wait("vector", "ac")
    p.emit("vector", lambda e: e.reciprocal(c_rsq2048[:, :], c_rsq2048[:, :]))
    p.emit("vector", lambda e: e.reciprocal(c_rsq8192[:, :], c_rsq8192[:, :]),
           incs=[("dvs", 1)])
    p.wait("vector", "sq", setup_ld[1])
    p.emit("vector", lambda e: e.tensor_tensor(c_supc[:, :], c_wsup[:, :], c_sudp[:, :], ALU.mult))
    p.wait("vector", "dvs")
    p.emit("vector", lambda e: e.tensor_scalar(c_supc[:, :], c_supc[:, :], c_rsq8192[:, 0:1],
                                               None, ALU.mult), incs=[("dv", 1)])

    g_down_mms = None
    g_store = None
    g_upg = None
    g_wscdp_reads = None
    wa_rel = [None, None]
    wf_rel = [None, None]

    def tap(name, src_ap, block=False):
        if name in tap_d and g == 0:
            p.wait("gpsimd", "dv")
            p.wait("gpsimd", "ac")
            p.wait("gpsimd", "pe")
            dma("gpsimd", tap_d[name], src_ap, sem="aq")
            if block:
                p.wait("vector", "aq")
                p.wait("tensor", "aq")

    for g in range(NG):
        p.wait_tok("sync", g_down_mms)
        x_src = x_d[g * G:(g + 1) * G, :].rearrange("(tb q) f -> q tb f", q=128)
        x1_ld = dma("sync", X1[:, :, :], x_src)

        p.wait("tensor", "sq", x1_ld[1])
        first_evac = True
        for tb in range(NTB):
            for icg in range(4):
                b = (tb * 4 + icg) % 2
                p.wait_tok("tensor", bank_tok[b])
                for k in range(4):
                    ic = icg * 4 + k
                    p.emit("tensor",
                           lambda e, tb=tb, ic=ic, b=b, k=k: e.transpose(
                               PB[b][:, k * 128:(k + 1) * 128],
                               X1[:, tb, ic * 128:(ic + 1) * 128], c_eye[:, :]),
                           incs=[("pe", 1)] if k == 3 else ())
                t = p.tok("pe")
                p.wait("vector", "pe", t[1])
                if first_evac:
                    p.wait_tok("vector", g_store)
                    first_evac = False
                p.emit("vector",
                       lambda e, tb=tb, icg=icg, b=b: e.tensor_copy(
                           X1T[:, icg * 4:icg * 4 + 4, tb * 128:(tb + 1) * 128], PB[b][:, :]),
                       incs=[("dv", 1)])
                bank_tok[b] = p.tok("dv")
        x1t_done = p.tok("dv")

        p.wait("tensor", "dv", x1t_done[1])
        p.wait_tok("tensor", bank_tok[2])
        for tb in range(NTB):
            for ic in range(FC):
                p.emit("tensor",
                       lambda e, tb=tb, ic=ic: e.matmul(
                           PB[2][:, tb * 16:(tb + 1) * 16],
                           X1T[:, ic, tb * 128:(tb + 1) * 128], c_wb[:, ic, :],
                           start=(ic == 0), stop=(ic == FC - 1)),
                       incs=[("pe", 1)] if (ic == FC - 1 and tb == NTB - 1) else ())
        p.wait("vector", "pe")
        p.emit("vector", lambda e: e.tensor_tensor(BTMP, PB[2][:, 0:64], c_bb4[:, :], ALU.add),
               incs=[("dv", 1)])
        bank_tok[2] = p.tok("dv")
        p.wait("scalar", "dv")
        p.emit("scalar", lambda e: e.activation(BTMP, BTMP, AF.Exp))
        p.emit("scalar", lambda e: e.activation(BETA, BTMP, AF.Ln, bias=1.0), incs=[("ac", 1)])
        tap("beta", BETA)

        def load_attn(h):
            s = h % 2
            for t_ in (wa_rel[s] or []):
                p.wait_tok("sync", t_)
            qv = V(WA[s], FC, 384)
            ov = V(WA[s] + 6144, FC, 128)
            bv = V(WA[s] + 8192, 384)
            dma("sync", qv[:, :, :], wqkv_d[h])
            dma("sync", ov[:, :, :], wog_d[h])
            ld = dma("sync", bv, bqkv_d[h])
            return ld, qv, ov, bv

        slot_info = {0: load_attn(0), 1: load_attn(1)}
        inv_rdy = [None, None]
        gate_rdy = [None, None]
        qkv_free = [None, None]
        gate_free = [None, None]
        b7_free = None
        out_tok = [None, None]
        sm_bank_tok = [None, None]
        dtp_tok = [None, None]

        def proj_head(h):
            s = h % 2
            ld, qv, ov, bv = slot_info[s]
            p.wait("tensor", "sq", ld[1])
            for tb in range(NTB):
                b = tb % 2
                p.wait_tok("tensor", bank_tok[b])
                for ic in range(FC):
                    p.emit("tensor",
                           lambda e, tb=tb, ic=ic, b=b, qv=qv: e.matmul(
                               PB[b][:, 0:384], X1T[:, ic, tb * 128:(tb + 1) * 128],
                               qv[:, ic, :], start=(ic == 0), stop=(ic == FC - 1)),
                           incs=[("pe", 1)] if ic == FC - 1 else ())
                mt = p.tok("pe")
                p.wait("vector", "pe", mt[1])
                if tb == 0:
                    p.wait_tok("vector", qkv_free[s])
                p.emit("vector",
                       lambda e, tb=tb, b=b, bv=bv, s=s: e.tensor_tensor(
                           QKVH[s][:, :, tb, :], PB[b][:, 0:384],
                           bv.rearrange("q (a b) -> q a b", b=128), ALU.add),
                       incs=[("dv", 1)])
                bank_tok[b] = p.tok("dv")
            qkv_rdy = p.tok("dv")
            p.wait_tok("tensor", bank_tok[2])
            for ic in range(FC):
                p.emit("tensor",
                       lambda e, ic=ic, ov=ov: e.matmul(
                           PB[2][:, :], ov[:, ic, :], X1T[:, ic, :],
                           start=(ic == 0), stop=(ic == FC - 1)),
                       incs=[("pe", 1)] if ic == FC - 1 else ())
            og_t = p.tok("pe")
            wa_rel[s] = [og_t, qkv_rdy]
            p.wait("scalar", "pe", og_t[1])
            p.wait_tok("scalar", gate_free[s])
            p.emit("scalar", lambda e, s=s: e.activation(GATEH[s], PB[2][:, :], AF.Silu),
                   incs=[("ac", 1)])
            bank_tok[2] = p.tok("ac")
            gate_rdy[s] = p.tok("ac")
            p.wait("vector", "dv", qkv_rdy[1])
            p.emit("vector", lambda e, s=s: e.tensor_tensor(
                SQQ, QKVH[s][:, 0, :, :], QKVH[s][:, 0, :, :], ALU.mult))
            p.emit("vector", lambda e, s=s: e.tensor_tensor(
                SQK, QKVH[s][:, 1, :, :], QKVH[s][:, 1, :, :], ALU.mult))
            p.emit("vector", lambda e: e.reduce_sum(
                NRM[:, 0:4], SQQ.rearrange("p (a b) -> p a b", b=128), axis=AX.X))
            p.emit("vector", lambda e: e.reduce_sum(
                NRM[:, 4:8], SQK.rearrange("p (a b) -> p a b", b=128), axis=AX.X),
                incs=[("dv", 1)])
            p.wait("scalar", "dv")
            p.emit("scalar", lambda e, s=s: e.activation(INV[s][:, :], NRM[:, :], AF.Sqrt,
                                                         scale=1.0 / 128.0), incs=[("ac", 1)])
            p.wait("vector", "ac")
            p.emit("vector", lambda e, s=s: e.tensor_scalar(
                INV[s][:, :], INV[s][:, :], c_eps12[:, 0:1], c_s128[:, 0:1], ALU.max, ALU.mult),
                incs=[("dvs", 1)])
            p.wait("vector", "dvs")
            p.emit("vector", lambda e, s=s: e.reciprocal(INV[s][:, :], INV[s][:, :]),
                   incs=[("dvs", 1)])
            inv_rdy[s] = p.tok("dvs")

        def core_head(h):
            nonlocal b7_free
            s = h % 2
            p.wait("vector", "dvs", inv_rdy[s][1])
            for tb in range(NTB):
                b = 3 + tb // 2
                ds = tb % 2
                p.wait_tok("vector", dtp_tok[ds])
                p.emit("vector", lambda e, s=s, tb=tb, ds=ds: e.tensor_scalar(
                    DQ[ds], c_eye[:, :], INV[s][:, tb:tb + 1], None, ALU.mult))
                p.emit("vector", lambda e, s=s, tb=tb, ds=ds: e.tensor_scalar(
                    DK[ds], c_eye[:, :], INV[s][:, 4 + tb:5 + tb], None, ALU.mult),
                    incs=[("dv", 1)])
                dt = p.tok("dv")
                p.wait("tensor", "dv", dt[1])
                if tb % 2 == 0:
                    p.wait_tok("tensor", bank_tok[b])
                p.emit("tensor", lambda e, s=s, tb=tb, b=b, ds=ds: e.matmul(
                    PB[b][:, (tb % 2) * 256:(tb % 2) * 256 + 128],
                    QKVH[s][:, 0, tb, :], DQ[ds], start=True, stop=True))
                p.emit("tensor", lambda e, s=s, tb=tb, b=b, ds=ds: e.matmul(
                    PB[b][:, (tb % 2) * 256 + 128:(tb % 2) * 256 + 256],
                    QKVH[s][:, 1, tb, :], DK[ds], start=True, stop=True),
                    incs=[("pe", 1)])
                dtp_tok[ds] = p.tok("pe")
                if tb % 2 == 1:
                    t = p.tok("pe")
                    p.wait("vector", "pe", t[1])
                    tbb = tb - 1
                    dst = sub_ap(QNKN, tbb * 128, [[128, 2], [512, 2], [1, 128]])
                    src = sub_ap(PB[b][:, :], 0, [[256, 2], [128, 2], [1, 128]])
                    p.emit("vector", lambda e, dst=dst, src=src: e.tensor_copy(dst, src),
                           incs=[("dv", 1)])
                    bank_tok[b] = p.tok("dv")
            qnkn_rdy = p.tok("dv")
            p.wait("tensor", "dv", qnkn_rdy[1])
            pend_out = None
            for c in range(9):
                if c < 8:
                    off = (c % 2) * 64
                    sb = 5 + (c % 2)
                    p.wait_tok("tensor", sm_bank_tok[c % 2])
                    if c == 0:
                        p.wait_tok("tensor", b7_free)
                    p.emit("tensor", lambda e, c=c, off=off, sb=sb: e.matmul(
                        PB[sb][off:off + 64, 0:64],
                        QNKN[:, 1, c * 64:(c + 1) * 64], QNKN[:, 0, c * 64:(c + 1) * 64],
                        start=True, stop=True), incs=[("pe", 1)])
                    st_ = p.tok("pe")
                    p.wait("vector", "pe", st_[1])
                    p.wait_tok("vector", out_tok[c % 2])
                    p.emit("vector", lambda e, c=c, off=off, sb=sb, h=h: e.scalar_tensor_tensor(
                        SMT[off:off + 64, off:off + 64], PB[sb][off:off + 64, 0:64],
                        BETA[off:off + 64, (c // 2) * 16 + h:(c // 2) * 16 + h + 1],
                        c_maskT[off:off + 64, :], ALU.mult, ALU.mult), incs=[("dv", 1)])
                    sm_bank_tok[c % 2] = p.tok("dv")
                if pend_out is not None:
                    pc, poff_, ptok = pend_out
                    p.wait("tensor", "dv", ptok[1])
                    p.emit("tensor", lambda e, c=pc, off=poff_, s=s: e.matmul(
                        PB[7][:, c * 64:(c + 1) * 64],
                        QKVH[s][off:off + 64, 2, c // 2, :],
                        SMT[off:off + 64, off:off + 64], start=True, stop=True),
                        incs=[("pe", 1)])
                    out_tok[pc % 2] = p.tok("pe")
                if c < 8:
                    pend_out = (c, off, sm_bank_tok[c % 2])
            p.wait("vector", "pe", out_tok[1][1])
            p.wait("vector", "ac", gate_rdy[s][1])
            p.emit("vector", lambda e, h=h, s=s: e.tensor_tensor(
                GATED[:, h, :], PB[7][:, :], GATEH[s], ALU.mult), incs=[("dv", 1)])
            b7_free = p.tok("dv")
            qkv_free[s] = p.tok("dv")
            gate_free[s] = p.tok("dv")

        for h in range(NH):
            proj_head(h)
            if h == 0:
                tap("qkv0", QKVH[0][:, :, :, :])
                tap("inv0", INV[0][:, :])
            if h + 2 < NH:
                slot_info[h % 2] = load_attn(h + 2)
            if h >= 1:
                core_head(h - 1)
            if h == 1:
                tap("qnkn0", QNKN[:, :, :])
                tap("smt0", SMT[:, :])
        core_head(NH - 1)
        gated_done = p.tok("dv")
        tap("gated", GATED[:, :, :])

        p.wait_tok("sync", g_wscdp_reads)
        bo_ld = dma("sync", c_bosc[:, :], bo_d)

        def load_wo(ob):
            s = ob % 2
            for t_ in (wa_rel[s] or []):
                p.wait_tok("sync", t_)
            wv = V(WA[s], FC, 512)
            ld = dma("sync", wv[:, :, :], wo_d[ob])
            return ld, wv

        wo_info = {0: load_wo(0), 1: load_wo(1)}
        p.wait("tensor", "dv", gated_done[1])
        ln1_apply = []
        for ob in range(4):
            ld, wv = wo_info[ob % 2]
            p.wait("tensor", "sq", ld[1])
            for tb in range(NTB):
                b = (ob % 2) * 4 + tb
                p.wait_tok("tensor", bank_tok[b])
                for fc in range(FC):
                    p.emit("tensor", lambda e, tb=tb, fc=fc, b=b, wv=wv: e.matmul(
                        PB[b][:, :], GATED[:, fc, tb * 128:(tb + 1) * 128], wv[:, fc, :],
                        start=(fc == 0), stop=(fc == FC - 1)),
                        incs=[("pe", 1)] if fc == FC - 1 else ())
            ot = p.tok("pe")
            wa_rel[ob % 2] = [ot]
            if ob + 2 < 4:
                wo_info[ob % 2] = load_wo(ob + 2)
            p.wait("vector", "pe", ot[1])
            p.wait("vector", "sq", bo_ld[1])
            for tb in range(NTB):
                b = (ob % 2) * 4 + tb
                p.emit("vector", lambda e, tb=tb, ob=ob, b=b: e.tensor_tensor(
                    TMPE[tb % 2], PB[b][:, :], c_bosc[:, ob * 512:(ob + 1) * 512], ALU.add))
                p.emit("vector", lambda e, tb=tb, ob=ob: e.tensor_tensor(
                    X1[:, tb, ob * 512:(ob + 1) * 512], X1[:, tb, ob * 512:(ob + 1) * 512],
                    TMPE[tb % 2], ALU.add), incs=[("dv", 1)])
                bank_tok[b] = p.tok("dv")
        for tb in range(NTB):
            for i in range(4):
                p.emit("vector", lambda e, tb=tb, i=i: e.bn_stats(
                    BNST[:, i * 6:(i + 1) * 6], X1[:, tb, i * 512:(i + 1) * 512]),
                    incs=[("dvs", 1)])
            p.wait("vector", "dvs")
            p.emit("vector", lambda e: e.bn_aggr(MV[:, :], BNST[:, :]), incs=[("dvs", 1)])
            p.wait("vector", "dvs")
            p.emit("vector", lambda e: e.tensor_copy(DUMW, MV[:, 1:2]), incs=[("dv", 1)])
            p.wait("scalar", "dv")
            p.emit("scalar", lambda e: e.activation(SD, MV[:, 1:2], AF.Sqrt, bias=c_eps[:, 0:1]),
                   incs=[("ac", 1)])
            p.wait("vector", "ac")
            p.emit("vector", lambda e: e.reciprocal(SINV, SD), incs=[("dvs", 1)])
            p.wait("vector", "dvs")
            p.emit("vector", lambda e: e.tensor_scalar(
                NEGMS, MV[:, 0:1], SINV[:, 0:1], c_neg1[:, 0:1], ALU.mult, ALU.mult),
                incs=[("dvs", 1)])
            p.wait("vector", "dvs")
            p.emit("vector", lambda e, tb=tb: e.tensor_scalar(
                X2[:, tb, :], X1[:, tb, :], SINV[:, 0:1], NEGMS[:, 0:1], ALU.mult, ALU.add),
                incs=[("dv", 1)])
            ln1_apply.append(p.tok("dv"))
        tap("x2", X2[:, :, :])

        p.wait("sync", "dv")
        wscdp_ld = dma("sync", c_bosc[:, :], wscdp_d)

        for ic in range(FC):
            b = 4 + ic % 2
            p.wait_tok("tensor", bank_tok[b])
            for tb in range(NTB):
                p.wait("tensor", "dv", ln1_apply[tb][1])
                p.emit("tensor", lambda e, ic=ic, tb=tb, b=b: e.transpose(
                    PB[b][:, tb * 128:(tb + 1) * 128], X2[:, tb, ic * 128:(ic + 1) * 128],
                    c_eye[:, :]), incs=[("pe", 1)] if tb == NTB - 1 else ())
            t = p.tok("pe")
            p.wait("vector", "pe", t[1])
            p.emit("vector", lambda e, ic=ic, b=b: e.tensor_scalar(
                X2T[:, ic, :], PB[b][:, :], c_suug[:, ic:ic + 1], c_rsq2048[:, 0:1],
                ALU.mult, ALU.mult), incs=[("dv", 1)])
            bank_tok[b] = p.tok("dv")
        tap("x2t", X2T[:, :, :])

        h_free = None
        down_last = None

        def x3h(sl):
            return V(X3_OFF[sl], 128).bitcast(BF16)

        def x3l(sl):
            return V(X3_OFF[sl] + 128, 128).bitcast(BF16)

        def hh(sl):
            return V(H_OFF[sl], 128).bitcast(BF16)

        def hl(sl):
            return V(H_OFF[sl] + 128, 128).bitcast(BF16)

        wf_store = [None, None]

        def bfly_dual(slotmap, tiles, strides, nlog, sp0, sp1, ndve):
            sp_d, sp_g = sp0, sp1
            for si, stride in enumerate(strides):
                p.wait("gpsimd", "dv")
                if si > 0:
                    p.wait("vector", "gp")
                pairs = [(base + j, base + j + stride)
                         for base in range(0, nlog, 2 * stride) for j in range(stride)]
                nd = min(ndve, len(pairs))
                for pi, (a_l, b_l) in enumerate(pairs):
                    dve = pi < nd
                    a_s, b_s = slotmap[a_l], slotmap[b_l]
                    sp = sp_d if dve else sp_g
                    last_d = dve and (pi == nd - 1 or pi == len(pairs) - 1)
                    last_g = (not dve) and pi == len(pairs) - 1
                    if dve:
                        p.emit("vector", lambda e, a=a_s, b=b_s, sp=sp: e.tensor_tensor(
                            tiles[sp], tiles[a], tiles[b], ALU.add))
                        p.emit("vector", lambda e, a=a_s, b=b_s: e.tensor_tensor(
                            tiles[a], tiles[a], tiles[b], ALU.subtract),
                            incs=[("dv", 1)] if pi == nd - 1 else ())
                    else:
                        p.emit("gpsimd", lambda e, a=a_s, b=b_s, sp=sp: e.tensor_add(
                            tiles[sp], tiles[a], tiles[b]))
                        p.emit("gpsimd", lambda e, a=a_s, b=b_s: e.tensor_sub(
                            tiles[a], tiles[a], tiles[b]),
                            incs=[("gp", 1)] if pi == len(pairs) - 1 else ())
                    slotmap[a_l], slotmap[b_l] = sp, a_s
                    if dve:
                        sp_d = b_s
                    else:
                        sp_g = b_s
                if len(pairs) <= nd:
                    p.emit("gpsimd", lambda e: e.tensor_copy(tiles[sp_g][:, 0:1],
                                                             tiles[sp_g][:, 0:1]),
                           incs=[("gp", 1)])
            p.wait("vector", "gp")
            p.wait("scalar", "gp")
            return sp_d, sp_g

        def split_load(s_, hbm_ap, nblk, blkf, scratch_ap, spbase):
            wb_ = O_X1 + s_ * 4096
            dma("sync", V(wb_, 4096), hbm_ap)
            ldt = p.tok("sq")
            p.wait("scalar", "sq", ldt[1])
            dests = [spbase] + [wb_ + k * blkf for k in range(nblk - 1)]
            subtok = {}
            for k in range(nblk):
                src = V(wb_ + k * blkf, blkf)
                hi = V(dests[k], blkf // 2).bitcast(BF16)
                lo = V(dests[k] + blkf // 2, blkf // 2).bitcast(BF16)
                if k >= 1:
                    p.wait_tok("scalar", subtok[k - 1])
                p.emit("scalar", lambda e, hi=hi, src=src: e.copy(hi, src), incs=[("ac", 1)])
                ct = p.tok("ac")
                p.wait("vector", "ac", ct[1])
                p.emit("vector", lambda e, lo=lo, src=src, hi=hi: e.tensor_tensor(
                    lo, src, hi, ALU.subtract), incs=[("dv", 1)])
                subtok[k] = p.tok("dv")
            pairs_tok = p.tok("dv")
            p.wait("scalar", "dv", pairs_tok[1])
            dma("scalar", scratch_ap[:, 0:blkf], V(spbase, blkf), sem="aq")
            dma("scalar", scratch_ap[:, blkf:nblk * blkf], V(wb_, (nblk - 1) * blkf),
                sem="aq")
            wf_store[s_] = p.tok("aq")
            return pairs_tok, dests

        for tf in range(NTF):
            ffn_first = (g == 0 and tf == 0)
            x3map = list(range(16)); x3spare = 16
            pend = None
            for ic in range(FC + 1):
                if ic < FC:
                    b = 4 + ic % 2
                    p.wait_tok("tensor", bank_tok[b])
                    p.emit("tensor", lambda e, ic=ic, tf=tf, b=b: e.matmul(
                        PB[b][:, 0:TF], c_hpm[:, :], X2T[:, ic, tf * TF:(tf + 1) * TF],
                        start=True, stop=True), incs=[("pe", 1)])
                    t = p.tok("pe")
                if pend is not None:
                    pic, pb_ = pend
                    p.wait("vector", "pe", t[1])
                    if pic == 0:
                        p.wait_tok("vector", g_upg)
                    p.emit("vector", lambda e, pic=pic, pb_=pb_: e.tensor_copy(
                        X3[pic], PB[pb_][:, 0:TF]), incs=[("dv", 1)])
                    bank_tok[pb_] = p.tok("dv")
                if ic < FC:
                    pend = (ic, b)
            for stride in (8, 4, 2, 1):
                for base in range(0, 16, stride * 2):
                    for j in range(stride):
                        a_l, b_l = base + j, base + j + stride
                        a_s, b_s = x3map[a_l], x3map[b_l]
                        sp = x3spare
                        p.emit("vector", lambda e, a_s=a_s, b_s=b_s, sp=sp: e.tensor_tensor(
                            X3[sp], X3[a_s], X3[b_s], ALU.add))
                        p.emit("vector", lambda e, a_s=a_s, b_s=b_s: e.tensor_tensor(
                            X3[a_s], X3[a_s], X3[b_s], ALU.subtract))
                        x3map[a_l], x3map[b_l], x3spare = sp, a_s, b_s
            p.emit("vector", lambda e: e.tensor_copy(DUMW, c_s128[:, 0:1]),
                   incs=[("dv", 1)])
            p.wait("scalar", "dv")
            p.wait_tok("scalar", g_upg)
            spq = [x3spare, 17]
            subtok = {}
            for lc in range(16):
                cs = x3map[lc]; sp = spq.pop(0)
                if lc >= 2:
                    p.wait_tok("scalar", subtok[lc - 2])
                p.emit("scalar", lambda e, cs=cs, sp=sp: e.copy(x3h(sp), X3[cs]),
                       incs=[("ac", 1)])
                ct = p.tok("ac")
                p.wait("vector", "ac", ct[1])
                p.emit("vector", lambda e, cs=cs, sp=sp: e.tensor_tensor(
                    x3l(sp), X3[cs], x3h(sp), ALU.subtract), incs=[("dv", 1)])
                subtok[lc] = p.tok("dv")
                x3map[lc] = sp
                spq.append(cs)
            x3_done = p.tok("dv")
            if "x3p" in tap_d and g == 0 and tf == 0:
                p.wait("gpsimd", "dv")
                for lc_ in range(16):
                    dma("gpsimd", tap_d["x3p"][lc_ * 128:(lc_ + 1) * 128, :],
                        V(X3_OFF[x3map[lc_]], 256), sem="aq")
                p.wait("vector", "aq")
                p.wait("scalar", "aq")
                p.wait("tensor", "aq")

            def load_wug(uc):
                s_ = uc % 2
                for t_ in (wf_rel[s_] or []):
                    p.wait_tok("sync", t_)
                wb_ = O_X1 + s_ * 4096
                if not ffn_first:
                    ld = dma("sync", V(wb_, 4096), wug_s[uc])
                    return ld, [wb_ + k * 256 for k in range(16)]
                return split_load(s_, wug_d[uc], 16, 256, wug_s[uc], WSP + s_ * 256)

            wug_info = {0: load_wug(0), 1: load_wug(1)}
            sg_tok = [None, None]
            h_rdy = []
            p.wait("tensor", "dv", x3_done[1])
            hm = list(range(64)); hspare = 64
            for uc in range(UC):
                ld, poff = wug_info[uc % 2]
                b = 6 + uc % 2

                def uph(ic, poff=poff):
                    return V(poff[ic], 64).bitcast(BF16)

                def gth(ic, poff=poff):
                    return V(poff[ic] + 64, 64).bitcast(BF16)

                def upl(ic, poff=poff):
                    return V(poff[ic] + 128, 64).bitcast(BF16)

                def gtl(ic, poff=poff):
                    return V(poff[ic] + 192, 64).bitcast(BF16)

                p.wait_tok("tensor", ld)
                p.wait_tok("tensor", bank_tok[b])
                if uc == 0:
                    p.wait_tok("tensor", h_free)
                for ic in range(FC):
                    xh_, xl_ = x3h(x3map[ic]), x3l(x3map[ic])
                    p.emit("tensor", lambda e, w=uph(ic), x=xh_, b=b, st=(ic == 0): e.matmul(
                        PB[b][:, 0:TF], w, x, start=st, stop=False))
                    p.emit("tensor", lambda e, w=uph(ic), x=xl_, b=b: e.matmul(
                        PB[b][:, 0:TF], w, x, start=False, stop=False))
                    p.emit("tensor", lambda e, w=upl(ic), x=xh_, b=b,
                           sp=(ic == FC - 1): e.matmul(
                        PB[b][:, 0:TF], w, x, start=False, stop=sp))
                for ic in range(FC):
                    xh_, xl_ = x3h(x3map[ic]), x3l(x3map[ic])
                    p.emit("tensor", lambda e, w=gth(ic), x=xh_, b=b, st=(ic == 0): e.matmul(
                        PB[b][:, TF:2 * TF], w, x, start=st, stop=False))
                    p.emit("tensor", lambda e, w=gth(ic), x=xl_, b=b: e.matmul(
                        PB[b][:, TF:2 * TF], w, x, start=False, stop=False))
                    p.emit("tensor", lambda e, w=gtl(ic), x=xh_, b=b,
                           sp=(ic == FC - 1): e.matmul(
                        PB[b][:, TF:2 * TF], w, x, start=False, stop=sp),
                        incs=[("pe", 1)] if ic == FC - 1 else ())
                mt = p.tok("pe")
                wf_rel[uc % 2] = [mt] + ([wf_store[uc % 2]] if wf_store[uc % 2] else [])
                wf_store[uc % 2] = None
                if uc + 2 < UC:
                    wug_info[uc % 2] = load_wug(uc + 2)
                p.wait("scalar", "pe", mt[1])
                p.wait_tok("scalar", sg_tok[uc % 2])
                p.emit("scalar", lambda e, uc=uc, b=b: e.activation(
                    SG[uc % 2], PB[b][:, TF:2 * TF], AF.Silu, scale=c_wsgt[:, uc:uc + 1]),
                    incs=[("ac", 1)])
                sg_rdy = p.tok("ac")
                p.wait("vector", "ac", sg_rdy[1])
                p.wait("vector", "pe", mt[1])
                p.emit("vector", lambda e, uc=uc, b=b: e.scalar_tensor_tensor(
                    H_[uc], PB[b][:, 0:TF], c_supc[:, uc:uc + 1], SG[uc % 2],
                    ALU.mult, ALU.mult), incs=[("dv", 1)])
                bank_tok[b] = p.tok("dv")
                sg_tok[uc % 2] = p.tok("dv")
                h_rdy.append(p.tok("dv"))
            g_upg = p.tok("pe")

            free = [64, 65, 66]
            hm2 = [None] * 64
            for pi in range(32):
                c0, c1 = pi, pi + 32
                bA = 4 + 2 * (pi % 2)
                bB = bA + 1
                p.wait_tok("tensor", bank_tok[bA])
                p.wait_tok("tensor", bank_tok[bB])
                p.wait_tok("tensor", h_rdy[c1])
                p.emit("tensor", lambda e, c=c0, b=bA: e.matmul(
                    PB[b][:, 0:TF], c_hpm[:, :], H_[c], start=True, stop=True))
                p.emit("tensor", lambda e, c=c1, b=bB: e.matmul(
                    PB[b][:, 0:TF], c_hpm[:, :], H_[c], start=True, stop=True),
                    incs=[("pe", 1)])
                t = p.tok("pe")
                sX, sY, sZ = free[0], free[1], free[2]
                p.wait("vector", "pe", t[1])
                p.emit("vector", lambda e, sX=sX, b=bA: e.tensor_copy(H_[sX], PB[b][:, 0:TF]))
                p.emit("vector", lambda e, sY=sY, sX=sX, b=bB: e.tensor_tensor(
                    H_[sY], PB[b][:, 0:TF], H_[sX], ALU.add))
                p.emit("vector", lambda e, sZ=sZ, sX=sX, b=bB: e.tensor_tensor(
                    H_[sZ], H_[sX], PB[b][:, 0:TF], ALU.subtract), incs=[("dv", 1)])
                bank_tok[bA] = p.tok("dv")
                bank_tok[bB] = p.tok("dv")
                hm2[c0], hm2[c1] = sY, sZ
                free = [c0, c1, sX]
            hm = hm2
            hspare = free[0]
            free_b = free[1]
            for stride in (16, 8, 4, 2, 1):
                for base in range(0, 64, stride * 2):
                    for j in range(stride):
                        a_l, b_l = base + j, base + j + stride
                        a_s, b_s = hm[a_l], hm[b_l]
                        sp = hspare
                        p.emit("vector", lambda e, a_s=a_s, b_s=b_s, sp=sp: e.tensor_tensor(
                            H_[sp], H_[a_s], H_[b_s], ALU.add))
                        p.emit("vector", lambda e, a_s=a_s, b_s=b_s: e.tensor_tensor(
                            H_[a_s], H_[a_s], H_[b_s], ALU.subtract))
                        hm[a_l], hm[b_l], hspare = sp, a_s, b_s
            p.emit("vector", lambda e: e.tensor_copy(DUMW, c_s128[:, 0:1]),
                   incs=[("dv", 1)])
            down_fp32 = (g == 0 and tf == 0)
            wdn_first = (g == 0 and tf == 1)
            p.wait("scalar", "dv")
            spq2 = [hspare, free_b]
            subtok2 = {}
            for lc in (range(0) if down_fp32 else range(64)):
                cs = hm[lc]; sp = spq2.pop(0)
                if lc >= 2:
                    p.wait_tok("scalar", subtok2[lc - 2])
                p.emit("scalar", lambda e, cs=cs, sp=sp: e.copy(hh(sp), H_[cs]),
                       incs=[("ac", 1)])
                ct = p.tok("ac")
                p.wait("vector", "ac", ct[1])
                p.emit("vector", lambda e, cs=cs, sp=sp: e.tensor_tensor(
                    hl(sp), H_[cs], hh(sp), ALU.subtract), incs=[("dv", 1)])
                subtok2[lc] = p.tok("dv")
                hm[lc] = sp
                spq2.append(cs)
            h_done = p.tok("dv")

            def load_wdn(idx):
                s_ = idx % 2
                for t_ in (wf_rel[s_] or []):
                    p.wait_tok("sync", t_)
                wb_ = O_X1 + s_ * 4096
                if down_fp32:
                    ld = dma("sync", V(wb_, 4096), wdn_d[idx // 8, idx % 8])
                    return ld, [wb_ + k * 512 for k in range(8)]
                if not wdn_first:
                    ld = dma("sync", V(wb_, 4096), wdn_s[idx])
                    return ld, [wb_ + k * 512 for k in range(8)]
                return split_load(s_, wdn_d[idx // 8, idx % 8], 8, 512, wdn_s[idx],
                                  O_QKV + s_ * 512)

            if "hpre" in tap_d and g == 0 and tf == 0:
                p.wait("gpsimd", "dv")
                for uc_ in range(UC):
                    dma("gpsimd", tap_d["hpre"][uc_ * 128:(uc_ + 1) * 128, :], H_[hm[uc_]],
                        sem="aq")
                p.wait("vector", "aq")
                p.wait("scalar", "aq")
                p.wait("tensor", "aq")
            wdn_info = {0: load_wdn(0), 1: load_wdn(1)}
            p.wait("tensor", "dv", h_done[1])
            for ob in range(4):
                for tb in range(2):
                    p.wait_tok("tensor", bank_tok[tb * 2 + (ob % 2)])
                for wave in range(8):
                    idx = ob * 8 + wave
                    ld, poff = wdn_info[idx % 2]

                    def dnh(k, poff=poff):
                        return V(poff[k], 256).bitcast(BF16)

                    def dnl(k, poff=poff):
                        return V(poff[k] + 256, 256).bitcast(BF16)

                    p.wait_tok("tensor", ld)
                    for tb in range(2):
                        for k in range(8):
                            uc = wave * 8 + k
                            b = tb * 2 + (ob % 2)
                            if down_fp32:
                                p.emit("tensor", lambda e, b=b,
                                       lh=H_[hm[uc]][:, tb * 128:(tb + 1) * 128],
                                       r=V(poff[k], 512), st=(uc == 0),
                                       sp=(uc == UC - 1): e.matmul(
                                    PB[b][:, :], lh, r, start=st, stop=sp),
                                    incs=[("pe", 1)] if (k == 7 and tb == 1) else ())
                                continue
                            lhH = hh(hm[uc])[:, tb * 128:(tb + 1) * 128]
                            lhL = hl(hm[uc])[:, tb * 128:(tb + 1) * 128]
                            p.emit("tensor", lambda e, lh=lhH, r=dnh(k), b=b,
                                   st=(uc == 0): e.matmul(
                                PB[b][:, :], lh, r, start=st, stop=False))
                            p.emit("tensor", lambda e, lh=lhH, r=dnl(k), b=b: e.matmul(
                                PB[b][:, :], lh, r, start=False, stop=False))
                            p.emit("tensor", lambda e, lh=lhL, r=dnh(k), b=b,
                                   sp=(uc == UC - 1): e.matmul(
                                PB[b][:, :], lh, r, start=False, stop=sp),
                                incs=[("pe", 1)] if (k == 7 and tb == 1) else ())
                    wf_rel[idx % 2] = [p.tok("pe")] + (
                        [wf_store[idx % 2]] if wf_store[idx % 2] else [])
                    wf_store[idx % 2] = None
                    if idx + 2 < 32:
                        wdn_info[idx % 2] = load_wdn(idx + 2)
                dt = p.tok("pe")
                p.wait("vector", "pe", dt[1])
                p.wait("vector", "sq", wscdp_ld[1])
                for tb in range(2):
                    b = tb * 2 + (ob % 2)
                    p.emit("vector", lambda e, tb=tb, ob=ob, b=b: e.tensor_tensor(
                        TMPE[tb], PB[b][:, :], c_bosc[:, ob * 512:(ob + 1) * 512], ALU.mult))
                    p.emit("vector", lambda e, tb=tb, ob=ob, tf=tf: e.tensor_tensor(
                        X2[:, tf * 2 + tb, ob * 512:(ob + 1) * 512],
                        X2[:, tf * 2 + tb, ob * 512:(ob + 1) * 512], TMPE[tb], ALU.add),
                        incs=[("dv", 1)])
                    bank_tok[b] = p.tok("dv")
            down_last = p.tok("pe")
            h_free = down_last
            g_wscdp_reads = p.tok("dv")
            if "r2" in tap_d and g == 0 and tf == 0:
                p.wait("gpsimd", "dv")
                dma("gpsimd", tap_d["r2"], sub_ap(X2, 0, [[2048, 2], [1, 2048]]), sem="aq")
                p.wait("vector", "aq")
                p.wait("scalar", "aq")

            for tb in range(2):
                for i in range(4):
                    p.emit("vector", lambda e, tb=tb, tf=tf, i=i: e.bn_stats(
                        BNST[:, i * 6:(i + 1) * 6],
                        X2[:, tf * 2 + tb, i * 512:(i + 1) * 512]), incs=[("dvs", 1)])
                p.wait("vector", "dvs")
                p.emit("vector", lambda e: e.bn_aggr(MV[:, :], BNST[:, :]), incs=[("dvs", 1)])
                p.wait("vector", "dvs")
                p.emit("vector", lambda e: e.tensor_copy(DUMW, MV[:, 1:2]), incs=[("dv", 1)])
                p.wait("scalar", "dv")
                p.emit("scalar", lambda e: e.activation(SD, MV[:, 1:2], AF.Sqrt,
                                                        bias=c_eps[:, 0:1]), incs=[("ac", 1)])
                p.wait("vector", "ac")
                p.emit("vector", lambda e: e.reciprocal(SINV, SD), incs=[("dvs", 1)])
                p.wait("vector", "dvs")
                p.emit("vector", lambda e: e.tensor_scalar(
                    NEGMS, MV[:, 0:1], SINV[:, 0:1], c_neg1[:, 0:1], ALU.mult, ALU.mult),
                    incs=[("dvs", 1)])
                p.wait("vector", "dvs")
                p.emit("vector", lambda e, tb=tb, tf=tf: e.tensor_scalar(
                    X2[:, tf * 2 + tb, :], X2[:, tf * 2 + tb, :], SINV[:, 0:1],
                    NEGMS[:, 0:1], ALU.mult, ALU.add), incs=[("dv", 1)])
                st_t = p.tok("dv")
                p.wait("scalar", "dv", st_t[1])
                row0 = g * G + tf * TF + tb * 128
                dma("scalar", y_d[row0:row0 + 128, :], X2[:, tf * 2 + tb, :], sem="aq")
        g_down_mms = down_last
        g_store = p.tok("aq")

    p.wait("gpsimd", "aq")
    p.emit("gpsimd", lambda e: e.nop())

    with nc.Block() as block:
        @block.sync
        def _(e):
            for f in p.ops["sync"]:
                f(e)

        @block.tensor
        def _(e):
            for f in p.ops["tensor"]:
                f(e)

        @block.vector
        def _(e):
            for f in p.ops["vector"]:
                f(e)

        @block.scalar
        def _(e):
            for f in p.ops["scalar"]:
                f(e)

        @block.gpsimd
        def _(e):
            for f in p.ops["gpsimd"]:
                f(e)

    es.close()
    return nc


def _prep_inputs(kw):
    f32 = np.float32
    hs = np.ascontiguousarray(np.asarray(kw["L0_attn_hidden_states"], f32).reshape(T * NCORE, DM))

    parts = []
    for nm in ("L0_attn_q_w", "L0_attn_k_w", "L0_attn_v_w"):
        WT = np.asarray(kw[nm], f32).T
        parts.append(WT.reshape(FC, 128, NH, 128).transpose(2, 1, 0, 3))
    wqkv = np.ascontiguousarray(np.concatenate(parts, axis=3))
    bq = []
    for h in range(NH):
        bq.append(np.concatenate([np.asarray(kw["L0_attn_q_b"], f32)[h * 128:(h + 1) * 128],
                                  np.asarray(kw["L0_attn_k_b"], f32)[h * 128:(h + 1) * 128],
                                  np.asarray(kw["L0_attn_v_b"], f32)[h * 128:(h + 1) * 128]]))
    bqkv = np.ascontiguousarray(np.broadcast_to(np.stack(bq)[:, None, :], (NH, 128, 384)))
    wog = np.ascontiguousarray(np.asarray(kw["L0_attn_og_w"], f32).T
                               .reshape(FC, 128, NH, 128).transpose(2, 1, 0, 3))
    wo = np.ascontiguousarray(np.asarray(kw["L0_attn_o_w"], f32).T
                              .reshape(FC, 128, 4, 512).transpose(2, 1, 0, 3))
    wb = np.ascontiguousarray(np.asarray(kw["L0_attn_b_w"], f32).T.reshape(FC, 128, 16)
                              .transpose(1, 0, 2))
    up = np.asarray(kw["L0_ffn_W_up"], f32).reshape(UC, 128, FC, 128).transpose(0, 3, 2, 1)
    gt = np.asarray(kw["L0_ffn_W_gate"], f32).reshape(UC, 128, FC, 128).transpose(0, 3, 2, 1)
    wug = np.ascontiguousarray(np.stack([up, gt], axis=3))
    wdn = np.ascontiguousarray(np.asarray(kw["L0_ffn_W_down"], f32).T
                               .reshape(8, 8, 128, 4, 512).transpose(3, 0, 2, 1, 4))
    bo = np.ascontiguousarray(np.broadcast_to(np.asarray(kw["L0_attn_o_b"], f32), (128, DM)))
    wscdp = np.ascontiguousarray(np.broadcast_to(np.asarray(kw["L0_ffn_Wscale_dp"], f32),
                                                 (128, DM)))
    bb4 = np.ascontiguousarray(np.broadcast_to(np.tile(np.asarray(kw["L0_attn_b_b"], f32), 4),
                                               (128, 64)))
    suug = np.ascontiguousarray(np.asarray(kw["L0_ffn_SU_ug"], f32).reshape(16, 128).T)
    wsup = np.ascontiguousarray(np.asarray(kw["L0_ffn_Wscale_ug"], f32)[:IN].reshape(64, 128).T)
    wsgt = np.ascontiguousarray(np.asarray(kw["L0_ffn_Wscale_ug"], f32)[IN:].reshape(64, 128).T)
    sudp = np.ascontiguousarray(np.asarray(kw["L0_ffn_SU_dp"], f32).reshape(64, 128).T)
    hpm = _hadamard128()
    eye = np.eye(128, dtype=f32)
    maskT = np.ascontiguousarray(np.tile(np.triu(np.ones((64, 64), f32)), (2, 1)))

    shared = dict(wqkv=wqkv, bqkv=bqkv, wog=wog, wo=wo, wb=wb, wug=wug, wdn=wdn,
                  bo=bo, wscdp=wscdp, bb4=bb4, suug=suug, wsup=wsup, wsgt=wsgt,
                  sudp=sudp, hpm=hpm, eye=eye, maskT=maskT)
    in_maps = []
    for c in range(NCORE):
        m = dict(shared)
        m["x"] = np.ascontiguousarray(hs[c * T:(c + 1) * T])
        in_maps.append(m)
    return in_maps


def kernel(**kw):
    in_maps = _prep_inputs(kw)
    nc = build_nc()
    res = run_bass_kernel_spmd(nc, in_maps, core_ids=list(range(NCORE)))
    y = np.concatenate([np.asarray(r["y"], np.float32) for r in res.results], axis=0)
    return y.reshape(2, 4096, DM)
